# version 11
# baseline (speedup 1.0000x reference)
import numpy as np
import ml_dtypes
from contextlib import ExitStack

import concourse.bass as bass
import concourse.mybir as mybir
from concourse.bass_utils import run_bass_kernel_spmd
from concourse.bass_isa import ReduceOp

F32 = mybir.dt.float32
BF16 = mybir.dt.bfloat16
ALU = mybir.AluOpType
AF = mybir.ActivationFunctionType
AXX = mybir.AxisListType.X

C23 = float(1.5 * 2**23)
C52 = float(1.5 * 2**52)

M = 2048
D = 1024
H = 16
DH = 4096
NMT = M // 128
NDC = D // 128
NCC = DH // 128
NEGMASK = -1.0e9

PHASES = 4
DEBUG_TAPS = ()
TRACE_ALLOC = False


def _ts(nc, guard, name):
    if TRACE_ALLOC:
        print(f"  alloc {name}: base={nc.sbuf_base} top={nc.sbuf_top} "
              f"free={nc.sbuf_top - nc.sbuf_base}")
    return guard


def _host_tables():
    i = np.arange(D // 2, dtype=np.float32)
    theta = (np.float32(10000.0) ** ((np.float32(-2.0) * (i - np.float32(1.0)))
                                     / np.float32(D))).astype(np.float32)
    ang = (np.arange(M, dtype=np.float32)[:, None] * theta[None, :]).astype(np.float32)
    cosT = np.ascontiguousarray(np.cos(ang).astype(np.float32).T)
    sinT = np.ascontiguousarray(np.sin(ang).astype(np.float32).T)
    return cosT, sinT


def _host_consts():
    k_idx = np.arange(128)
    maskT = np.where(k_idx[:, None] <= k_idx[None, :], np.float32(0.0),
                     np.float32(NEGMASK)).astype(np.float32)
    ident32 = np.eye(128, dtype=np.float32)
    identbf = np.eye(128, dtype=ml_dtypes.bfloat16)
    ones11 = np.ones((1, 1), dtype=np.float32)
    sel2 = np.zeros((2, 128), dtype=np.float32)
    sel2[0, :64] = 1.0
    sel2[1, 64:] = 1.0
    return maskT, ident32, identbf, ones11, sel2


class Prog:
    ENGS = ("sync", "tensor", "vector", "scalar", "gpsimd")
    OWNER = {"v": "vector", "s": "scalar", "t": "tensor", "g": "gpsimd"}

    def __init__(self, nc, es):
        self.nc = nc
        self.es = es
        self.streams = {e: [] for e in self.ENGS}
        self.sems = {}
        self.cnt = {}
        self.last_real = {e: -1 for e in self.ENGS}

    def sem(self, name):
        if name not in self.sems:
            self.sems[name] = self.es.enter_context(self.nc.semaphore(name))
            self.cnt[name] = 0
        return self.sems[name]

    def run(self, eng, build):
        self.streams[eng].append((build, []))
        self.last_real[eng] = len(self.streams[eng]) - 1

    def inc(self, eng, build, sem_name, amt):
        s = self.sem(sem_name)
        self.cnt[sem_name] += amt
        val = self.cnt[sem_name]
        self.streams[eng].append((build, [[s, amt, s, val]]))
        self.last_real[eng] = len(self.streams[eng]) - 1
        return val

    def mark(self, eng, sem_name, amt=1):
        s = self.sem(sem_name)
        self.cnt[sem_name] += amt
        val = self.cnt[sem_name]
        i = self.last_real[eng]
        build, updates = self.streams[eng][i]
        for u in updates:
            if u[0] is s:
                u[1] += amt
                u[3] = val
                break
        else:
            if updates:
                ps, pv = updates[0][2], updates[0][3]
                self.streams[eng].append(
                    (lambda E, a_=ps, b_=pv: E.wait_ge(a_, b_), [[s, amt, s, val]]))
            else:
                updates.append([s, amt, s, val])
        return val

    def wait(self, eng, sem_name, val=None):
        if val is None:
            val = self.cnt.get(sem_name, 0)
        if val <= 0:
            return
        if self.OWNER.get(sem_name) == eng:
            return
        s = self.sem(sem_name)
        self.streams[eng].append((lambda E, ss=s, v=val: E.wait_ge(ss, v), None))

    def val(self, sem_name):
        self.sem(sem_name)
        return self.cnt[sem_name]

    def serialize(self, eng):
        i = self.last_real[eng]
        build, updates = self.streams[eng][i]
        if updates:
            ss, val = updates[0][2], updates[0][3]
        else:
            name = "ch_" + eng
            ss = self.sem(name)
            self.cnt[name] += 1
            val = self.cnt[name]
            updates.append([ss, 1, ss, val])
        self.streams[eng].append((lambda E, s_=ss, v=val: E.wait_ge(s_, v), None))
        return val

    def barrier(self):
        snapshot = dict(self.cnt)
        for eng in self.ENGS:
            for name, v in snapshot.items():
                self.wait(eng, name, v)


def build_nc(debug_taps=(), phases=PHASES):
    nc = bass.Bass("TRN2", target_bir_lowering=False,
                   detect_race_conditions=False)
    es = ExitStack()
    p = Prog(nc, es)

    def din(name, shape, dt=F32):
        return nc.dram_tensor(name, list(shape), dt, kind="ExternalInput").ap()

    x_d = din("x", (M, D))
    wT_d = {"q": din("wqT", (D, D)), "k": din("wkT", (D, D)), "v": din("wvT", (D, D))}
    wiT_d = {"q": din("wqiT", (D, D)), "k": din("wkiT", (D, D)), "v": din("wviT", (D, D))}
    bqi_d = din("d_bqi", (128, NDC)); bki_d = din("d_bki", (128, NDC)); bvi_d = din("d_bvi", (D,))
    owT_d = din("owT", (D, D)); ob_d = din("d_ob", (D,))
    w1T_d = din("w1T", (D, DH)); w2T_d = din("w2T", (DH, D))
    convw_d = din("d_convw", (128, 3 * NCC)); convb_d = din("d_convb", (128, NCC))
    ropec_d = din("ropec", (D // 2, M)); ropes_d = din("ropes", (D // 2, M))
    maskT_d = din("d_maskT", (128, 128))
    id32_d = din("d_id32", (128, 128)); idbf_d = din("d_idbf", (128, 128), BF16)
    ones11_d = din("d_ones11", (1, 1)); sel2_d = din("d_sel2", (2, 128))
    ones1r_d = din("d_ones1r", (1, 128)); onescol_d = din("d_onescol", (128, 1))
    ones1rb_d = din("d_ones1rb", (1, 128), BF16)

    y_d = nc.dram_tensor("y", [M, D], F32, kind="ExternalOutput").ap()
    xqT_scr = nc.dram_tensor("xqT_scr", [D, M], BF16, kind="Internal").ap()
    h2s_scr = nc.dram_tensor("h2s_scr", [DH, M], F32, kind="Internal").ap()

    taps = {}
    def tap(name, shape, dt=F32):
        if name in debug_taps:
            taps[name] = nc.dram_tensor("tap_" + name, list(shape), dt,
                                        kind="ExternalOutput").ap()
            return taps[name]
        return None

    def sb(name, shape, dt=F32):
        g = es.enter_context(nc.sbuf_tensor(name, list(shape), dt))
        return _ts(nc, g, name)

    maskT = sb("maskT", (128, 128))
    id32 = sb("id32", (128, 128))
    idbf = sb("idbf", (128, 128), BF16)
    ones1r = sb("ones1r", (1, 128))
    ones1rb = sb("ones1rb", (1, 128), BF16)
    ones_col = sb("ones_col", (128, 1))
    bqik = {"q": sb("bqi", (128, NDC)), "k": sb("bki", (128, NDC))}
    bviB = sb("bviB", (128, D), BF16)
    obB = sb("obB", (128, D), BF16)
    convw = sb("convw", (128, 3 * NCC))
    convb = sb("convb", (128, NCC))
    halfc = sb("halfc", (128, 1))
    ddet = sb("ddet", (128, 1))
    w1det = sb("w1det", (128, 1))
    w2det = sb("w2det", (128, 1))
    cst = {}
    inv1_row = sb("inv1_row", (1, M))
    wts = {}
    for nmm in ("w1s", "w1si", "w2s", "w2si"):
        wts[nmm] = sb(nmm, (128, 1))

    pA = es.enter_context(nc.psum_tensor("pA", [128, 2048], F32))
    pB = es.enter_context(nc.psum_tensor("pB", [128, 2048], F32))

    SYNCQ = "sync"
    STOREQ = "scalar"
    CASTQ = "gpsimd"

    for (dst, src) in ((maskT, maskT_d), (id32, id32_d), (idbf, idbf_d),
                       (ones1r, ones1r_d), (ones_col, onescol_d),
                       (ones1rb, ones1rb_d)):
        p.inc(SYNCQ, lambda E, d=dst, s=src: E.dma_start(out=d[:], in_=s[:]), "sLC", 16)
    p.inc(SYNCQ, lambda E: E.dma_start(out=bqik["q"][:], in_=bqi_d[:]), "sLC", 16)
    p.inc(SYNCQ, lambda E: E.dma_start(out=bqik["k"][:], in_=bki_d[:]), "sLC", 16)
    p.inc(CASTQ, lambda E: E.dma_start(out=bviB[0:1, :], in_=bvi_d[None, :]), "gB", 16)
    gB = p.inc(CASTQ, lambda E: E.dma_start(out=obB[0:1, :], in_=ob_d[None, :]), "gB", 16)
    p.inc(SYNCQ, lambda E: E.dma_start(out=convw[:], in_=convw_d[:]), "sLC", 16)
    p.inc(SYNCQ, lambda E: E.dma_start(out=convb[:], in_=convb_d[:]), "sLC", 16)
    cLC = p.val("sLC")

    p.run("vector", lambda E: E.memset(halfc[:], 0.5))
    p.serialize("vector")
    p.run("vector", lambda E: E.tensor_scalar(ddet[:], halfc[:], C23, -C23, ALU.add, ALU.add))
    p.serialize("vector")
    p.run("vector", lambda E: E.tensor_scalar(w2det[:], ddet[:], 2.0, None, ALU.mult))
    p.serialize("vector")
    p.inc("vector", lambda E: E.tensor_scalar(w1det[:], ddet[:], -2.0, 1.0,
                                              ALU.mult, ALU.add), "v", 1)
    p.serialize("vector")
    for nm_, vv_ in (("c23c", C23), ("n23c", -C23), ("c52c", C52), ("n52c", -C52)):
        cst[nm_] = sb(nm_, (128, 1))
        p.run("vector", lambda E, tt=cst[nm_], v=vv_: E.memset(tt[:], v))
    p.mark("vector", "v")
    p.wait("tensor", "sLC", cLC)
    p.wait("tensor", "gB", gB)
    for bi, bt in enumerate((bviB, obB)):
        for seg in range(2):
            p.run("tensor", lambda E, b=bt, s_=seg, i=bi: E.matmul(
                pA[:, i * 1024 + s_ * 512:i * 1024 + (s_ + 1) * 512],
                ones1rb[:], b[0:1, s_ * 512:(s_ + 1) * 512],
                start=True, stop=True))
    tBC = p.mark("tensor", "t")
    p.wait("vector", "t", tBC)
    for bi, bt in enumerate((bviB, obB)):
        p.run("vector", lambda E, b=bt, i=bi: E.tensor_copy(
            b[:], pA[:, i * 1024:(i + 1) * 1024]))
    vBC = p.mark("vector", "v")
    p.wait("tensor", "v", vBC)

    es1 = ExitStack()
    sb1 = lambda name, shape, dt=F32: es1.enter_context(
        nc.sbuf_tensor(name, list(shape), dt))
    xall = [sb1(f"xall{t}", (128, D)) for t in range(NMT)]
    sXs = []
    for mt_ in range(NMT):
        sXs.append(p.inc(CASTQ, lambda E, b=xall[mt_], m=mt_: E.dma_start(
            out=b[:], in_=x_d[m * 128:(m + 1) * 128, :]), "sX", 16))

    es0b = ExitStack()
    sb0b = lambda name, shape, dt=F32: _ts(nc, es0b.enter_context(
        nc.sbuf_tensor(name, list(shape), dt)), name)
    wmb = [sb0b(f"wmb{i}", (128, DH)) for i in range(2)]
    wab = [sb0b(f"wab{i}", (128, DH)) for i in range(2)]
    wabs = {"w1": sb0b("wabs1", (128, 1)), "w2": sb0b("wabs2", (128, 1))}
    wtmp = sb0b("wtmp", (128, 1))
    for key, dram, nch in (("w1", w1T_d, NDC), ("w2", w2T_d, NCC)):
        w = dram.shape[1]
        for ch in range(nch):
            buf = wmb[ch % 2]
            abuf = wab[ch % 2]
            p.wait(SYNCQ, "s")
            p.inc(SYNCQ, lambda E, b=buf, d=dram, c=ch, ww=w: E.dma_start(
                out=b[:, 0:ww], in_=d[c * 128:(c + 1) * 128, :]), "sLW", 16)
            p.wait("scalar", "sLW")
            p.wait("scalar", "v")
            p.inc("scalar", lambda E, b=buf, a=abuf, ww=w: E.activation(
                a[:, 0:ww], b[:, 0:ww], AF.Abs), "s", 1)
            p.wait("vector", "s")
            if ch == 0:
                p.inc("vector", lambda E, a=wabs[key], ab=abuf, ww=w: E.reduce_sum(
                    a[:], ab[:, 0:ww], AXX), "v", 1)
            else:
                p.inc("vector", lambda E, ab=abuf, ww=w: E.reduce_sum(
                    wtmp[:], ab[:, 0:ww], AXX), "v", 1)
                p.serialize("vector")
                p.inc("vector", lambda E, a=wabs[key]: E.tensor_add(a[:], a[:], wtmp[:]),
                      "v", 1)
                p.serialize("vector")
    vWR = p.val("v")
    p.wait("tensor", "v", vWR)
    p.run("tensor", lambda E: E.matmul(pB[0:1, 0:1], ones_col[:], wabs["w1"][:],
                                       start=True, stop=True))
    tWR = p.inc("tensor", lambda E: E.matmul(pB[0:1, 512:513], ones_col[:],
                                             wabs["w2"][:], start=True, stop=True),
                "t", 1)
    wsum = {"w1": wabs["w1"], "w2": wabs["w2"]}
    p.wait("vector", "t", tWR)
    p.run("vector", lambda E: E.tensor_copy(wabs["w1"][0:1, :], pB[0:1, 0:1]))
    p.serialize("vector")
    p.run("vector", lambda E: E.tensor_copy(wabs["w2"][0:1, :], pB[0:1, 512:513]))
    p.serialize("vector")
    for key in ("w1", "w2"):
        p.run("vector", lambda E, a=wabs[key]: E.tensor_scalar(
            a[0:1, :], a[0:1, :], 1.0 / (D * DH), 1e-5, ALU.mult, ALU.max))
        p.serialize("vector")
        p.run("vector", lambda E, a=wabs[key]: E.reciprocal(a[0:1, :], a[0:1, :]))
        p.serialize("vector")
    vWS = p.mark("vector", "v")
    p.wait("tensor", "v", vWS)
    p.run("tensor", lambda E: E.matmul(pB[0:128, 1024:1025], ones1r[:],
                                       wabs["w1"][0:1, :], start=True, stop=True))
    tWS = p.inc("tensor", lambda E: E.matmul(pB[0:128, 1536:1537], ones1r[:],
                                             wabs["w2"][0:1, :], start=True, stop=True),
                "t", 1)
    p.wait("vector", "t", tWS)
    p.run("vector", lambda E: E.tensor_copy(wts["w1s"][:], pB[:, 1024:1025]))
    p.serialize("vector")
    p.run("vector", lambda E: E.tensor_copy(wts["w2s"][:], pB[:, 1536:1537]))
    p.serialize("vector")
    p.run("vector", lambda E: E.reciprocal(wts["w1si"][:], wts["w1s"][:]))
    p.serialize("vector")
    p.run("vector", lambda E: E.reciprocal(wts["w2si"][:], wts["w2s"][:]))
    p.serialize("vector")
    vWSS = p.mark("vector", "v")
    p.wait("tensor", "v", vWSS)
    es0b.close()

    es_xnT = ExitStack()
    xnT = [es_xnT.enter_context(nc.sbuf_tensor(f"xnT{c}", [128, M], BF16, side="right"))
           for c in range(NDC)]
    if True:
        xq32 = [sb1(f"xq32_{i}", (128, D)) for i in range(2)]
        xqTst = [sb1(f"xqTst{c}", (128, M), BF16) for c in range(NDC)]
        sqd = [sb1(f"sqd{i}", (128, D)) for i in range(2)]
        mvA = sb1("mvA", (128, NMT)); mvB = sb1("mvB", (128, NMT))
        s0 = sb1("s0", (128, NMT)); r0 = sb1("r0", (128, NMT)); r1 = sb1("r1", (128, NMT))
        va = sb1("va", (128, NMT)); tN = sb1("tN", (128, NMT)); nmb = sb1("nmb", (128, NMT))
        amax = sb1("amax", (128, NMT)); rc_ = sb1("rc", (128, NMT))
        sc1 = sb1("sc1", (128, NMT)); iv1 = sb1("iv1", (128, NMT))
        sw1 = sb1("sw1", (128, NMT)); sw2 = sb1("sw2", (128, NMT))
        rta = sb1("rta", (128, D)); rtb = sb1("rtb", (128, D)); rth = sb1("rth", (128, D))

        t_xn32 = tap("xn32", (M, D))
        t_xqbf = tap("xqbf", (M, D))
        t_inv1 = tap("inv1", (1, M))

        vRed = [0] * NMT
        for mt in range(NMT):
            par = mt % 2
            p.wait("scalar", "sX", sXs[mt])
            if mt >= 2:
                p.wait("scalar", "v", vRed[mt - 2])
            sD = p.inc("scalar", lambda E, b=xall[mt], q=sqd[par]: E.activation(
                q[:], b[:], AF.Square), "s", 1)
            p.wait("vector", "sX", sXs[mt])
            p.run("vector", lambda E, b=xall[mt], m=mt: E.reduce_sum(
                mvA[:, m:m + 1], b[:], AXX))
            p.wait("vector", "s", sD)
            vRed[mt] = p.inc("vector", lambda E, q=sqd[par], m=mt: E.reduce_sum(
                mvB[:, m:m + 1], q[:], AXX), "v", 1)
        p.serialize("vector")
        p.run("vector", lambda E: E.tensor_scalar(mvA[:], mvA[:], 1.0 / D, None, ALU.mult))
        p.serialize("vector")
        p.run("vector", lambda E: E.tensor_scalar(mvB[:], mvB[:], 1.0 / D, None, ALU.mult))
        p.serialize("vector")
        p.run("vector", lambda E: E.tensor_mul(va[:], mvA[:], mvA[:]))
        p.serialize("vector")
        p.run("vector", lambda E: E.tensor_tensor(va[:], mvB[:], va[:], ALU.subtract))
        p.serialize("vector")
        vVA = p.inc("vector", lambda E: E.tensor_scalar(
            va[:], va[:], 1e-5, None, ALU.add), "v", 1)
        p.serialize("vector")
        p.wait("scalar", "v", vVA)
        sSQ = p.inc("scalar", lambda E: E.activation(s0[:], va[:], AF.Sqrt), "s", 1)
        p.wait("vector", "s", sSQ)
        p.run("vector", lambda E: E.reciprocal(r0[:], s0[:]))
        p.serialize("vector")
        p.run("vector", lambda E: E.tensor_mul(tN[:], r0[:], r0[:]))
        p.serialize("vector")
        p.run("vector", lambda E: E.tensor_mul(tN[:], tN[:], va[:]))
        p.serialize("vector")
        p.run("vector", lambda E: E.tensor_scalar(tN[:], tN[:], -0.5, 1.5, ALU.mult, ALU.add))
        p.serialize("vector")
        p.run("vector", lambda E: E.tensor_mul(r1[:], r0[:], tN[:]))
        p.serialize("vector")
        p.run("vector", lambda E: E.tensor_scalar(nmb[:], mvA[:], -1.0, None, ALU.mult))
        p.serialize("vector")
        vRS = p.inc("vector", lambda E: E.tensor_mul(nmb[:], nmb[:], r1[:]), "v", 1)
        p.serialize("vector")
        p.wait("scalar", "v", vRS)
        sXN = [0] * NMT
        vMX = [0] * NMT
        for mt in range(NMT):
            par = mt % 2
            sXN[mt] = p.inc("scalar", lambda E, b=xall[mt], m=mt: E.activation(
                b[:], b[:], AF.Identity, bias=nmb[:, m:m + 1],
                scale=r1[:, m:m + 1]), "s", 1)
            if t_xn32 is not None:
                p.wait(STOREQ, "s", sXN[mt])
                p.inc(STOREQ, lambda E, i=xall[mt], m=mt: E.dma_start(
                    out=t_xn32[m * 128:(m + 1) * 128, :], in_=i[:]), "stD", 16)
            if mt >= 2:
                p.wait("scalar", "v", vMX[mt - 2])
            sAB = p.inc("scalar", lambda E, i=xall[mt], q=sqd[par]: E.activation(
                q[:], i[:], AF.Abs), "s", 1)
            p.wait("vector", "s", sAB)
            vMX[mt] = p.inc("vector", lambda E, q=sqd[par], m=mt: E.reduce_max(
                amax[:, m:m + 1], q[:], AXX), "v", 1)
            p.wait("tensor", "s", sXN[mt])
            if mt >= 1:
                p.wait("tensor", "s", vXnEv[mt - 1])
            for dc in range(NDC):
                p.run("tensor", lambda E, s=xall[mt], d=dc: E.transpose(
                    pA[:, d * 128:(d + 1) * 128], s[:, d * 128:(d + 1) * 128],
                    id32[:]))
            tT = p.mark("tensor", "t")
            p.wait("scalar", "t", tT)
            for dc in range(NDC):
                p.run("scalar", lambda E, d=dc, m=mt:
                      E.activation(xnT[d][:, m * 128:(m + 1) * 128],
                                   pA[:, d * 128:(d + 1) * 128], AF.Identity))
            if mt == 0:
                vXnEv = [0] * NMT
            vXnEv[mt] = p.mark("scalar", "s")
        p.serialize("vector")
        p.run("vector", lambda E: E.tensor_scalar(amax[:], amax[:], 1e-5, None, ALU.max))
        p.serialize("vector")
        p.run("vector", lambda E: E.reciprocal(rc_[:], amax[:]))
        p.serialize("vector")
        p.run("vector", lambda E: E.tensor_scalar(sc1[:], rc_[:], 127.0, None, ALU.mult))
        p.serialize("vector")
        p.run("vector", lambda E: E.reciprocal(iv1[:], sc1[:]))
        p.serialize("vector")
        p.run("vector", lambda E: E.tensor_scalar(sw1[:], sc1[:], w1det[:], None, ALU.mult))
        p.serialize("vector")
        p.run("vector", lambda E: E.tensor_scalar(sw2[:], sc1[:], w2det[:], None, ALU.mult))
        p.serialize("vector")
        vXqEv = [0] * NMT
        for mt in range(NMT):
            par = mt % 2
            if mt >= 2:
                p.wait("vector", "t", tXq[mt - 2] if mt >= 2 else 0)
            p.run("vector", lambda E, i=xall[mt], m=mt: E.tensor_scalar(
                rta[:], i[:], sw1[:, m:m + 1], C23, ALU.mult, ALU.add))
            p.run("vector", lambda E, i=xall[mt], m=mt: E.tensor_scalar(
                rtb[:], i[:], sw2[:, m:m + 1], C52, ALU.mult, ALU.add))
            p.run("vector", lambda E: E.tensor_scalar(rth[:], rta[:], -C23, None, ALU.add))
            vXQ = p.inc("vector", lambda E, o=xq32[par]: E.scalar_tensor_tensor(
                o[:], rtb[:], -C52, rth[:], ALU.add, ALU.add), "v", 1)
            if t_xqbf is not None:
                p.wait(STOREQ, "v", vXQ)
                p.inc(STOREQ, lambda E, o=xq32[par], m=mt: E.dma_start(
                    out=t_xqbf[m * 128:(m + 1) * 128, :], in_=o[:]), "stD", 16)
            p.wait("tensor", "v", vXQ)
            p.run("tensor", lambda E, m=mt: E.matmul(
                pB[0:1, m * 128:(m + 1) * 128], iv1[:, m:m + 1], id32[:],
                start=(m % 4 == 0), stop=(m % 4 == 3)))
            if mt >= 1:
                p.wait("tensor", "s", vXqEv[mt - 1])
            for dc in range(NDC):
                p.run("tensor", lambda E, s=xq32[par], d=dc: E.transpose(
                    pA[:, 1024 + d * 128:1024 + (d + 1) * 128],
                    s[:, d * 128:(d + 1) * 128], id32[:]))
            if mt == 0:
                tXq = [0] * NMT
            tXq[mt] = p.mark("tensor", "t")
            p.wait("scalar", "t", tXq[mt])
            for dc in range(NDC):
                p.run("scalar", lambda E, d=dc, m=mt:
                      E.activation(xqTst[d][:, m * 128:(m + 1) * 128],
                                   pA[:, 1024 + d * 128:1024 + (d + 1) * 128],
                                   AF.Identity))
            vXqEv[mt] = p.mark("scalar", "s")
        tIV = p.mark("tensor", "t")
        p.wait("vector", "t", tIV)
        vIV = p.inc("vector", lambda E: E.tensor_copy(inv1_row[:], pB[0:1, :]), "v", 1)
        if t_inv1 is not None:
            p.wait(STOREQ, "v", vIV)
            p.inc(STOREQ, lambda E: E.dma_start(out=t_inv1[:], in_=inv1_row[:]), "stD", 16)
        p.wait(STOREQ, "v")
        for dc in range(NDC):
            p.inc(STOREQ, lambda E, d=dc: E.dma_start(
                out=xqT_scr[d * 128:(d + 1) * 128, :], in_=xqTst[d][:]), "stQ", 16)
        p.barrier()
        es1.close()

    if "xnT" in debug_taps:
        t_xnT = tap("xnT", (D, M), BF16)
        for dc in range(NDC):
            p.inc(STOREQ, lambda E, d=dc, t=t_xnT: E.dma_start(
                out=t[d * 128:(d + 1) * 128, :], in_=xnT[d][:]), "stD", 16)

    if phases >= 2:
        attn_acc = sb("attn_acc", (128, NMT, D), BF16)
        p.es_qk = ExitStack()
        qh = [_ts(nc, p.es_qk.enter_context(nc.sbuf_tensor(f"qh{c}", [128, M], BF16)), f"qh{c}")
              for c in range(NDC)]
        kh = [_ts(nc, p.es_qk.enter_context(nc.sbuf_tensor(f"kh{c}", [128, M], BF16)), f"kh{c}")
              for c in range(NDC)]
        p.es_vh = ExitStack()
        late = {}

        with ExitStack() as es2:
            sb2 = lambda name, shape, dt=F32: _ts(nc, es2.enter_context(
                nc.sbuf_tensor(name, list(shape), dt)), name)
            es_tab = ExitStack()
            ropec = [es_tab.enter_context(nc.sbuf_tensor(
                f"ropec{c}", [128, M], BF16, side="right")) for c in range(2)]
            ropes = [es_tab.enter_context(nc.sbuf_tensor(
                f"ropes{c}", [128, M], BF16, side="right")) for c in range(2)]
            tab_loaded = {}

            def load_tables(c):
                par = c % 2
                p.wait(CASTQ, "v")
                p.inc(CASTQ, lambda E, cc=c, pp=par: E.dma_start(
                    out=ropec[pp][:], in_=ropec_d[cc * 128:(cc + 1) * 128, :]), "gT", 16)
                tab_loaded[c] = p.inc(CASTQ, lambda E, cc=c, pp=par: E.dma_start(
                    out=ropes[pp][:], in_=ropes_d[cc * 128:(cc + 1) * 128, :]), "gT", 16)
            wb2 = [sb2(f"wb2_{i}", (128, D), BF16) for i in range(2)]
            q0 = [sb2(f"q0_{i}", (128, M), BF16) for i in range(2)]
            qr = [sb2(f"qr{c}", (128, M), BF16) for c in range(NDC)]
            rt1 = sb2("rt1", (128, M), BF16)



            t_q0 = tap("q0", (D, M), BF16)
            t_qr = tap("qr", (D, M), BF16)
            t_qh = tap("qh", (D, M), BF16)
            t_kh = tap("kh", (D, M), BF16)
            t_vh = tap("vh", (128, NMT * H * 65), BF16)

            def load_wcol(dram, ic, wb, sem_name):
                return p.inc(CASTQ, lambda E, w=wb, d=dram, i=ic: E.dma_start(
                    out=w[:].rearrange("p (dc q) -> p dc q", dc=NDC),
                    in_=d.rearrange("(dc p) i -> p dc i", p=128)[:, :, i * 128:(i + 1) * 128]),
                    sem_name, 16)

            order = [0, 4, 1, 5, 2, 6, 3, 7]

            p.fm_tW = [0, 0]

            def fm_projection(wdram, moving, consume):
                p.wait("tensor", "v")
                last_tW = p.fm_tW
                for idx, ic in enumerate(order):
                    wb = wb2[idx % 2]
                    p.wait(CASTQ, "t")
                    gW = load_wcol(wdram, ic, wb, "gW")
                    p.wait("tensor", "gW", gW)
                    for mh in range(2):
                        ps = pA[:, mh * 1024:(mh + 1) * 1024]
                        p.wait("tensor", "s")
                        for dc in range(NDC):
                            for seg in range(2):
                                ml = mh * 1024 + seg * 512
                                p.run("tensor", lambda E, w=wb, d=dc, s_=seg, ml_=ml,
                                      pp=ps, first=(dc == 0), last=(dc == NDC - 1),
                                      mo=moving: E.matmul(
                                          pp[:, s_ * 512:(s_ + 1) * 512],
                                          w[:, d * 128:(d + 1) * 128],
                                          mo[d][:, ml_:ml_ + 512],
                                          start=first, stop=last))
                        tMM = p.mark("tensor", "t")
                        last_tW[idx % 2] = tMM
                        consume(ic, mh, ps, tMM)

            def run_qkv(kind):
                seen = {}
                tq0 = t_q0 if (kind == "q") else None

                def consume(ic, mh, ps, tMM):
                    par = 0 if ic < 4 else 1
                    p.wait("scalar", "t", tMM)
                    p.wait("scalar", "v")
                    sEV = p.inc("scalar", lambda E, q=q0[par], m=mh, pp=ps:
                                E.activation(q[:, m * 1024:(m + 1) * 1024], pp[:],
                                             AF.Identity), "s", 1)
                    p.evac_guard[mh] = sEV
                    if tq0 is not None:
                        p.wait(STOREQ, "s", sEV)
                        p.inc(STOREQ, lambda E, q=q0[par], i=ic, m=mh, t=tq0:
                              E.dma_start(
                                  out=t[i * 128:(i + 1) * 128, m * 1024:(m + 1) * 1024],
                                  in_=q[:, m * 1024:(m + 1) * 1024]), "stD", 16)
                    seen[(ic, mh)] = True
                    c = ic - 4
                    if par == 1 and mh == 1 and seen.get((c, 1)):
                        tp_ = c % 2
                        if c < 3:
                            load_tables(c + 1)
                        p.wait("vector", "s")
                        p.wait("vector", "t")
                        p.wait("vector", "gT", tab_loaded[c])
                        p.run("vector", lambda E, cc=c, tb=tp_: E.tensor_mul(qr[cc][:], q0[0][:], ropec[tb][:]))
                        p.run("vector", lambda E, tb=tp_: E.tensor_mul(rt1[:], q0[1][:], ropes[tb][:]))
                        p.run("vector", lambda E, cc=c: E.tensor_add(qr[cc][:], qr[cc][:], rt1[:]))
                        p.run("vector", lambda E, cc=c, tb=tp_: E.tensor_mul(qr[cc + 4][:], q0[1][:], ropec[tb][:]))
                        p.run("vector", lambda E, tb=tp_: E.tensor_mul(rt1[:], q0[0][:], ropes[tb][:]))
                        p.inc("vector", lambda E, cc=c: E.tensor_tensor(
                            qr[cc + 4][:], qr[cc + 4][:], rt1[:], ALU.subtract), "v", 1)

                load_tables(0)
                fm_projection(wT_d[kind], xnT, consume)
                vROPE = p.val("v")
                if kind == "q" and t_qr is not None:
                    p.wait(STOREQ, "v", vROPE)
                    for c in range(NDC):
                        p.inc(STOREQ, lambda E, cc=c, t=t_qr: E.dma_start(
                            out=t[cc * 128:(cc + 1) * 128, :], in_=qr[cc][:]), "stD", 16)

                if kind in ("q", "k"):
                    dstset = qh if kind == "q" else kh
                    bias = bqik[kind]

                    def consume2(ic, mh, ps, tMM):
                        p.wait("scalar", "t", tMM)
                        p.evac_guard[mh] = p.inc(
                            "scalar", lambda E, o=dstset[ic], m=mh, pp=ps,
                            b=bias, i=ic: E.activation(
                                o[:, m * 1024:(m + 1) * 1024], pp[:],
                                AF.Identity, bias=b[:, i:i + 1]), "s", 1)

                    fm_projection(wiT_d[kind], qr, consume2)
                    tp = t_qh if kind == "q" else t_kh
                    if tp is not None:
                        p.wait(STOREQ, "s")
                        for c in range(NDC):
                            p.inc(STOREQ, lambda E, cc=c, ds=dstset, t=tp: E.dma_start(
                                out=t[cc * 128:(cc + 1) * 128, :], in_=ds[cc][:]), "stD", 16)
                else:
                    es_tab.close()
                    es_xnT.close()
                    wv3 = sb2("wv3", (128, NDC * D), BF16)
                    vh_ext = p.es_vh.enter_context(
                        nc.sbuf_tensor("vh_ext", [128, NMT, H * 65], BF16, side="right"))
                    late["vh_ext"] = vh_ext
                    p.run("vector", lambda E: E.memset(vh_ext[:], 1.0))
                    vONES = p.mark("vector", "v")
                    p.wait(CASTQ, "v", vROPE)
                    gW = p.inc(CASTQ, lambda E: E.dma_start(
                        out=wv3[:].rearrange("p (dc i) -> p dc i", dc=NDC),
                        in_=wiT_d["v"].rearrange("(dc p) i -> p dc i", p=128)), "gW", 16)
                    p.wait("tensor", "gW", gW)
                    p.wait("tensor", "v", vROPE)
                    wv3v = wv3[:].rearrange("p (dc i) -> p dc i", dc=NDC)
                    last_vAD = [0, vONES]
                    for mt in range(NMT):
                        par2 = mt % 2
                        ps = pA[:, 0:1024] if par2 == 0 else pA[:, 1024:2048]
                        p.wait("tensor", "v", last_vAD[par2])
                        if mt < 2:
                            p.wait("tensor", "s", p.evac_guard.get(par2, 0))
                        for dc in range(NDC):
                            for seg in range(2):
                                p.run("tensor", lambda E, d=dc, s_=seg, m=mt, pp=ps,
                                      first=(dc == 0), last=(dc == NDC - 1):
                                      E.matmul(pp[:, s_ * 512:(s_ + 1) * 512],
                                               qr[d][:, m * 128:(m + 1) * 128],
                                               wv3v[:, d, s_ * 512:(s_ + 1) * 512],
                                               start=first, stop=last))
                        tMM = p.mark("tensor", "t")
                        p.wait("vector", "t", tMM)
                        if mt == 0:
                            p.wait("vector", "v", vONES)
                        last_vAD[par2] = p.inc("vector", lambda E, m=mt, pp=ps: E.tensor_add(
                            vh_ext[:, m, :].rearrange("p (h q) -> p h q", q=65)[:, :, 0:64],
                            pp.rearrange("p (h q) -> p h q", q=64),
                            bviB[:].rearrange("p (h q) -> p h q", q=64)), "v", 1)

            p.evac_guard = {}
            run_qkv("q")
            run_qkv("k")
            run_qkv("v")
            vh_ext = late["vh_ext"]
            if t_vh is not None:
                p.wait(STOREQ, "v")
                p.inc(STOREQ, lambda E, t=t_vh: E.dma_start(
                    out=t[:], in_=vh_ext[:].rearrange("p a b -> p (a b)")), "stD", 16)
            p.barrier()

    if phases >= 3:
        es_att = ExitStack()
        oT = _ts(nc, es_att.enter_context(
            nc.sbuf_tensor("oT", [128, NDC, M], BF16, side="right")), "oT")
        with ExitStack() as es3:
            sb3 = lambda name, shape, dt=F32: _ts(nc, es3.enter_context(
                nc.sbuf_tensor(name, list(shape), dt)), name)
            pTp = [sb3(f"pTp{i}", (128, 2048), BF16) for i in range(2)]
            zr0 = sb3("zr0", (1, 512)); zr1 = sb3("zr1", (1, 512))
            zb_sb = sb3("zb_sb", (128, 512))
            t_oT = tap("oT", (128, NDC * M), BF16)

            last_sEX = [0]
            last_PV = [0, 0]
            vEV_prev = [0]
            for hp in range(NDC):
                h0c = 65 * (2 * hp)
                h1c = 65 * (2 * hp + 1)
                for qc in range(4):
                    nt = 4 * qc + 4
                    qbase = qc * 512
                    sEX_of = {}

                    def emit_pv_pair(pk, h0c=h0c, h1c=h1c, qbase=qbase, nt=nt):
                        pp = pk % 2
                        if pk == 0 and vEV_prev[0]:
                            p.wait("tensor", "v", vEV_prev[0])
                        p.wait("tensor", "s", sEX_of[pk])
                        for ti in range(2):
                            t = 2 * pk + ti
                            off = max(0, t * 128 - qbase)
                            N = 512 - off
                            base = 1024 * ti
                            p.run("tensor", lambda E, tt=t, o=off, n=N, b=pTp[pp],
                                  bs=base, hc=h0c, first=(t == 0),
                                  last=(t == nt - 1):
                                  E.matmul(pB[0:65, o:512],
                                           vh_ext[:, tt, hc:hc + 65],
                                           b[:, bs + o:bs + o + n], start=first,
                                           stop=last, tile_position=(0, 0)))
                            p.run("tensor", lambda E, tt=t, o=off, n=N, b=pTp[pp],
                                  bs=base, hc=h1c, first=(t == 0),
                                  last=(t == nt - 1):
                                  E.matmul(pB[64:128, 512 + o:1024],
                                           vh_ext[:, tt, hc:hc + 64],
                                           b[:, bs + 512 + o:bs + 512 + o + n],
                                           start=first, stop=last,
                                           tile_position=(0, 64)))
                            last_PV[pp] = p.inc(
                                "tensor", lambda E, tt=t, o=off, n=N, b=pTp[pp],
                                bs=base, hc=h1c, first=(t == 0),
                                last=(t == nt - 1):
                                E.matmul(pB[0:1, 1024 + o:1536],
                                         vh_ext[:, tt, hc + 64:hc + 65],
                                         b[:, bs + 512 + o:bs + 512 + o + n],
                                         start=first, stop=last,
                                         tile_position=(0, 0)), "t", 1)

                    for pk in range(nt // 2):
                        pp = pk % 2
                        p.wait("tensor", "s", last_sEX[0])
                        tQK = 0
                        for ti in range(2):
                            t = 2 * pk + ti
                            sS0 = pA[:, ti * 1024:ti * 1024 + 512]
                            sS1 = pA[:, ti * 1024 + 512:ti * 1024 + 1024]
                            off = max(0, t * 128 - qbase)
                            N = 512 - off
                            qlo = qbase + off
                            p.run("tensor", lambda E, s_=sS0, o=off, n=N, ql=qlo,
                                  tt=t, h=hp: E.matmul(
                                      s_[:, o:o + n],
                                      kh[h][0:64, tt * 128:(tt + 1) * 128],
                                      qh[h][0:64, ql:ql + n],
                                      start=True, stop=True,
                                      tile_position=(0, 0)))
                            tQK = p.inc("tensor", lambda E, s_=sS1, o=off, n=N,
                                        ql=qlo, tt=t, h=hp: E.matmul(
                                            s_[:, o:o + n],
                                            kh[h][64:128, tt * 128:(tt + 1) * 128],
                                            qh[h][64:128, ql:ql + n],
                                            start=True, stop=True,
                                            tile_position=(64, 0)), "t", 1)
                        vMK = 0
                        for ti in range(2):
                            t = 2 * pk + ti
                            if t * 128 < qbase:
                                continue
                            sS0 = pA[:, ti * 1024:ti * 1024 + 512]
                            sS1 = pA[:, ti * 1024 + 512:ti * 1024 + 1024]
                            off = max(0, t * 128 - qbase)
                            if vMK == 0:
                                p.wait("vector", "t", tQK)
                            p.run("vector", lambda E, s_=sS0, o=off:
                                  E.tensor_add(s_[:, o:o + 128], s_[:, o:o + 128],
                                               maskT[:]))
                            vMK = p.inc("vector", lambda E, s_=sS1, o=off:
                                        E.tensor_add(s_[:, o:o + 128],
                                                     s_[:, o:o + 128], maskT[:]),
                                        "v", 1)
                        if vMK:
                            p.wait("scalar", "v", vMK)
                        else:
                            p.wait("scalar", "t", tQK)
                        p.wait("scalar", "t", last_PV[pp])
                        sEX = p.inc("scalar", lambda E, b=pTp[pp]: E.activation(
                            b[:, :], pA[:, :], AF.Exp, scale=0.125), "s", 1)
                        last_sEX[0] = sEX
                        sEX_of[pk] = sEX
                        if pk >= 1:
                            emit_pv_pair(pk - 1)
                    emit_pv_pair(nt // 2 - 1)
                    tPV = p.val("t")
                    p.wait("vector", "t", tPV)
                    p.run("vector", lambda E: E.reciprocal(zr0[:], pB[64:65, 0:512]))
                    vRC = p.inc("vector", lambda E: E.reciprocal(
                        zr1[:], pB[0:1, 1024:1536]), "v", 1)
                    p.wait("tensor", "v", vRC)
                    p.run("tensor", lambda E: E.matmul(
                        pB[0:64, 1536:2048], ones1r[0:1, 0:64], zr0[:],
                        start=True, stop=True))
                    tZB = p.inc("tensor", lambda E: E.matmul(
                        pB[64:128, 1536:2048], ones1r[0:1, 0:64], zr1[:],
                        start=True, stop=True, tile_position=(0, 64)), "t", 1)
                    p.wait("vector", "t", tZB)
                    p.run("vector", lambda E: E.tensor_copy(zb_sb[:], pB[:, 1536:2048]))
                    p.serialize("vector")
                    p.run("vector", lambda E, qb=qbase, h=hp: E.tensor_mul(
                        oT[0:64, h, qb:qb + 512], pB[0:64, 0:512], zb_sb[0:64, :]))
                    vEV_prev[0] = p.inc("vector", lambda E, qb=qbase, h=hp: E.tensor_mul(
                        oT[64:128, h, qb:qb + 512], pB[64:128, 512:1024],
                        zb_sb[64:128, :]), "v", 1)
            if t_oT is not None:
                p.wait(STOREQ, "v")
                p.inc(STOREQ, lambda E, t=t_oT: E.dma_start(
                    out=t[:], in_=oT[:].rearrange("p a b -> p (a b)")), "stD", 16)
            p.barrier()

        p.es_qk.close()
        with ExitStack() as es35:
            sb35 = lambda name, shape, dt=F32: _ts(nc, es35.enter_context(
                nc.sbuf_tensor(name, list(shape), dt)), name)
            ow3 = sb35("ow3", (128, NDC * D), BF16)
            gW = p.inc(CASTQ, lambda E: E.dma_start(
                out=ow3[:].rearrange("p (dc i) -> p dc i", dc=NDC),
                in_=owT_d.rearrange("(dc p) i -> p dc i", p=128)), "gW", 16)
            p.wait("tensor", "gW", gW)
            ow3v = ow3[:].rearrange("p (dc i) -> p dc i", dc=NDC)
            t_attn = tap("attn", (128, NMT * D), BF16)
            for mt in range(NMT):
                ps = pA[:, 0:1024] if mt % 2 == 0 else pA[:, 1024:2048]
                p.wait("tensor", "v")
                for dc in range(NDC):
                    for seg in range(2):
                        p.run("tensor", lambda E, d=dc, s_=seg, m=mt, pp=ps,
                              first=(dc == 0), last=(dc == NDC - 1):
                              E.matmul(pp[:, s_ * 512:(s_ + 1) * 512],
                                       oT[:, d, m * 128:(m + 1) * 128],
                                       ow3v[:, d, s_ * 512:(s_ + 1) * 512],
                                       start=first, stop=last))
                tMM = p.mark("tensor", "t")
                p.wait("vector", "t", tMM)
                p.inc("vector", lambda E, m=mt, pp=ps: E.tensor_add(
                    attn_acc[:, m, :], pp[:], obB[:]), "v", 1)
            if t_attn is not None:
                p.wait(STOREQ, "v")
                p.inc(STOREQ, lambda E, t=t_attn: E.dma_start(
                    out=t[:], in_=attn_acc[:].rearrange("p a b -> p (a b)")), "stD", 16)
            p.barrier()
        es_att.close()
        p.es_vh.close()

    if phases >= 4:
        with ExitStack() as es4a:
            sb4 = lambda name, shape, dt=F32: _ts(nc, es4a.enter_context(
                nc.sbuf_tensor(name, list(shape), dt)), name)
            xqr = [sb4(f"xqr{c}", (128, M), BF16) for c in range(NDC)]
            w1q = sb4("w1q", (128, NDC * DH), BF16)
            w1qv = w1q[:].rearrange("p (dc c) -> p dc c", dc=NDC)
            alphaB = sb4("alphaB", (128, M))
            macc = sb4("macc", (128, M))
            rhi = sb4("rhi", (1, M), BF16)
            rlo = sb4("rlo", (1, M), BF16)
            es4q = ExitStack()
            sb4q = lambda name, shape, dt=F32: _ts(nc, es4q.enter_context(
                nc.sbuf_tensor(name, list(shape), dt)), name)
            wst = [sb4q(f"wst{i}", (128, DH // 2)) for i in range(4)]
            thrP = sb4q("thrP", (128, 1)); thrN = sb4q("thrN", (128, 1))
            g1t = sb4q("g1t", (128, DH // 2))

            for dc in range(NDC):
                p.inc(SYNCQ, lambda E, d=dc: E.dma_start(
                    out=xqr[d][:], in_=xqT_scr[d * 128:(d + 1) * 128, :]), "sXQ", 16)
            sXQ = p.val("sXQ")
            p.run("vector", lambda E: E.tensor_scalar(thrP[:], wts["w1si"][:], 0.5, None, ALU.mult))
            p.serialize("vector")
            p.inc("vector", lambda E: E.tensor_scalar(thrN[:], thrP[:], -1.0, None, ALU.mult), "v", 1)
            p.serialize("vector")
            p.run("vector", lambda E: E.tensor_copy(rhi[:], inv1_row[:]))
            p.serialize("vector")
            vHL = p.inc("vector", lambda E: E.tensor_tensor(
                rlo[:], inv1_row[:], rhi[:], ALU.subtract), "v", 1)
            p.serialize("vector")
            p.wait("tensor", "v", vHL)
            for seg in range(4):
                p.run("tensor", lambda E, s_=seg: E.matmul(
                    pB[:, s_ * 512:(s_ + 1) * 512], ones1rb[:],
                    rhi[0:1, s_ * 512:(s_ + 1) * 512], start=True, stop=False))
                p.run("tensor", lambda E, s_=seg: E.matmul(
                    pB[:, s_ * 512:(s_ + 1) * 512], ones1rb[:],
                    rlo[0:1, s_ * 512:(s_ + 1) * 512], start=False, stop=True))
            tAB = p.mark("tensor", "t")
            p.wait("vector", "t", tAB)
            gAB = p.inc("vector", lambda E: E.tensor_scalar(
                alphaB[:], pB[:, :], wts["w1si"][:], -1.0, ALU.mult, ALU.mult), "v", 1)
            last_w4 = [0, 0, 0, 0]
            for dc in range(NDC):
                for hh in range(2):
                    k4 = (2 * dc + hh) % 4
                    buf = wst[k4]
                    p.wait(SYNCQ, "v", last_w4[k4])
                    sW = p.inc(SYNCQ, lambda E, b=buf, d=dc, h=hh: E.dma_start(
                        out=b[:], in_=w1T_d[d * 128:(d + 1) * 128,
                                           h * (DH // 2):(h + 1) * (DH // 2)]), "sW1", 16)
                    p.wait("vector", "sW1", sW)
                    p.run("vector", lambda E, b=buf: E.tensor_scalar(
                        g1t[:], b[:], thrP[:], None, ALU.is_gt))
                    last_w4[k4] = p.inc(
                        "vector", lambda E, b=buf, d=dc, h=hh: E.scalar_tensor_tensor(
                            w1qv[:, d, h * (DH // 2):(h + 1) * (DH // 2)],
                            b[:], thrN[:], g1t[:], ALU.is_lt, ALU.subtract), "v", 1)
            es4q.close()
            h_sb = [sb4(f"h_sb{i}", (128, M + 2)) for i in range(2)]
            s_sb = [sb4(f"s_sb{i}", (128, M)) for i in range(2)]
            g1t_ = sb4("g1t_", (128, M))
            ab_sb = g1t_
            c_sb = s_sb
            p.run("vector", lambda E: E.memset(macc[:], 0.0))
            for i in range(2):
                p.run("vector", lambda E, b=h_sb[i]: E.memset(b[:, 0:1], 0.0))
                p.run("vector", lambda E, b=h_sb[i]: E.memset(b[:, M + 1:M + 2], 0.0))
            vINIT = p.mark("vector", "v")
            p.wait("tensor", "v", vINIT)
            p.wait("tensor", "sXQ", sXQ)
            p.wait("tensor", "v", gAB)

            t_hsb = tap("hsb", (DH, M))
            t_ssb = tap("ssb", (DH, M))
            last_stH = [0, 0]
            last_vHE = [0, gAB]
            last_sC1 = [0, 0]
            last_sAB2 = [0]
            for cc in range(NCC):
                par = cc % 2
                ps = pA if par == 0 else pB
                p.wait("tensor", "v", last_vHE[par])
                for dc in range(NDC):
                    for seg in range(4):
                        p.run("tensor", lambda E, d=dc, s_=seg, c=cc, pp=ps,
                              first=(dc == 0), last=(dc == NDC - 1):
                              E.matmul(pp[:, s_ * 512:(s_ + 1) * 512],
                                       w1qv[:, d, c * 128:(c + 1) * 128],
                                       xqr[d][:, s_ * 512:(s_ + 1) * 512],
                                       start=first, stop=last))
                tMM = p.mark("tensor", "t")
                p.wait("vector", "t", tMM)
                p.wait("vector", "s", last_sC1[par])
                if t_hsb is not None:
                    p.wait("vector", "stD")
                vHE = p.inc("vector", lambda E, pp=ps, b=h_sb[par]: E.tensor_mul(
                    b[:, 1:M + 1], pp[:], alphaB[:]), "v", 1)
                last_vHE[par] = vHE
                if last_sAB2[0] > 0:
                    p.wait("vector", "s", last_sAB2[0])
                    p.inc("vector", lambda E: E.tensor_tensor(
                        macc[:], macc[:], ab_sb[:], ALU.max), "v", 1)
                    last_sAB2[0] = 0
                p.wait("scalar", "v", vHE)
                p.wait("scalar", "stH", last_stH[par])
                if t_ssb is not None:
                    p.wait("scalar", "stD")
                sC1 = p.inc("scalar", lambda E, b=h_sb[par], o=c_sb[par], c=cc:
                            E.activation(o[:], b[:, 0:M], AF.Identity,
                                         scale=convw[:, c:c + 1],
                                         bias=convb[:, c:c + 1]), "s", 1)
                last_sC1[par] = sC1
                p.wait("vector", "s", sC1)
                p.run("vector", lambda E, b=h_sb[par], o=c_sb[par], c=cc:
                      E.scalar_tensor_tensor(
                          g1t_[:], b[:, 1:M + 1], convw[:, NCC + c:NCC + c + 1],
                          o[:], ALU.mult, ALU.add))
                vC3 = p.inc("vector", lambda E, b=h_sb[par], o=c_sb[par], c=cc:
                            E.scalar_tensor_tensor(
                                o[:], b[:, 2:M + 2],
                                convw[:, 2 * NCC + c:2 * NCC + c + 1],
                                g1t_[:], ALU.mult, ALU.add), "v", 1)
                p.wait("scalar", "v", vC3)
                sSI = p.inc("scalar", lambda E, o=c_sb[par]:
                            E.activation(o[:], o[:], AF.Silu), "s", 1)
                sAB2 = p.inc("scalar", lambda E, s_=s_sb[par]: E.activation(
                    ab_sb[:], s_[:], AF.Abs), "s", 1)
                last_sAB2[0] = sAB2
                p.serialize("scalar")
                last_stH[par] = p.inc(STOREQ, lambda E, s_=s_sb[par], c=cc: E.dma_start(
                    out=h2s_scr[c * 128:(c + 1) * 128, :], in_=s_[:]), "stH", 16)
                if t_hsb is not None:
                    p.inc(STOREQ, lambda E, b=h_sb[par], c=cc, t=t_hsb: E.dma_start(
                        out=t[c * 128:(c + 1) * 128, :], in_=b[:, 1:M + 1]), "stD", 16)
                if t_ssb is not None:
                    p.inc(STOREQ, lambda E, s_=s_sb[par], c=cc, t=t_ssb: E.dma_start(
                        out=t[c * 128:(c + 1) * 128, :], in_=s_[:]), "stD", 16)
            p.wait("vector", "s", last_sAB2[0])
            p.inc("vector", lambda E: E.tensor_tensor(
                macc[:], macc[:], ab_sb[:], ALU.max), "v", 1)
            amax2c = _ts(nc, es.enter_context(nc.sbuf_tensor(
                "amax2c", [128, NMT], F32, side="right")), "amax2c")
            vMF = p.val("v")
            p.wait("tensor", "v", vMF)
            for mt in range(NMT):
                bk = mt % 4
                p.wait("tensor", "v")
                p.run("tensor", lambda E, m=mt, b=bk: E.transpose(
                    pB[:, b * 512:b * 512 + 128], macc[:, m * 128:(m + 1) * 128],
                    id32[:]))
                tTR = p.mark("tensor", "t")
                p.wait("vector", "t", tTR)
                p.inc("vector", lambda E, m=mt, b=bk: E.reduce_max(
                    amax2c[:, m:m + 1], pB[:, b * 512:b * 512 + 128], AXX), "v", 1)
                p.serialize("vector")
            p.barrier()

        with ExitStack() as es4b:
            sb4b = lambda name, shape, dt=F32: _ts(nc, es4b.enter_context(
                nc.sbuf_tensor(name, list(shape), dt)), name)
            w2q = sb4b("w2q", (128, NCC * D), BF16)
            w2qv = w2q[:].rearrange("p (cc i) -> p cc i", cc=NCC)
            sc1B = sb4b("sc1B", (128, M)); sc2B = sb4b("sc2B", (128, M))
            inv2c = sb4b("inv2c", (128, NMT)); a2cols = sb4b("a2cols", (128, NMT))
            sc2c = sb4b("sc2c", (128, NMT))
            h2l = [sb4b(f"h2l{i}", (128, 512)) for i in range(4)]
            u2 = sb4b("u2", (128, 2, 512))
            ra_ = sb4b("ra_", (128, 2, 512)); rg_ = sb4b("rg_", (128, 2, 512))
            h2q = sb4b("h2q", (128, NCC, 512), BF16)
            t1m = sb4b("t1m", (128, D)); t2m = t1m
            xre = [u2[:].rearrange("p b c -> p (b c)"),
                   ra_[:].rearrange("p b c -> p (b c)")]
            u1v = t1m[:].rearrange("p (b c) -> p b c", b=2)
            ysb = [sb4b(f"ysb{i}", (128, D)) for i in range(2)]
            es4w = ExitStack()
            sb4w = lambda name, shape, dt=F32: _ts(nc, es4w.enter_context(
                nc.sbuf_tensor(name, list(shape), dt)), name)
            w2st = [sb4w(f"w2st{i}", (128, D)) for i in range(2)]
            thr2P = sb4w("thr2P", (128, 1)); thr2N = sb4w("thr2N", (128, 1))
            g2t = sb4w("g2t", (128, D))

            p.run("vector", lambda E: E.tensor_scalar(thr2P[:], wts["w2si"][:], 0.5, None, ALU.mult))
            p.serialize("vector")
            p.inc("vector", lambda E: E.tensor_scalar(thr2N[:], thr2P[:], -1.0, None, ALU.mult), "v", 1)
            p.serialize("vector")
            last_w3 = [0, 0]
            for cc in range(NCC):
                k3 = cc % 2
                buf = w2st[k3]
                p.wait(SYNCQ, "v", last_w3[k3])
                sW = p.inc(SYNCQ, lambda E, b=buf, c=cc: E.dma_start(
                    out=b[:], in_=w2T_d[c * 128:(c + 1) * 128, :]), "sW2", 16)
                p.wait("vector", "sW2", sW)
                p.run("vector", lambda E, b=buf: E.tensor_scalar(
                    g2t[:], b[:], thr2P[:], None, ALU.is_gt))
                last_w3[k3] = p.inc(
                    "vector", lambda E, b=buf, c=cc: E.scalar_tensor_tensor(
                        w2qv[:, c, :], b[:], thr2N[:], g2t[:], ALU.is_lt,
                        ALU.subtract), "v", 1)
            es4w.close()
            es_row = ExitStack()
            sc2row = None
            if "sc2r" in debug_taps:
                sc2row = _ts(nc, es_row.enter_context(nc.sbuf_tensor(
                    "sc2row", [1, M], F32, side="right")), "sc2row")
            p.run("vector", lambda E: E.tensor_scalar(amax2c[:], amax2c[:], 1e-5, None, ALU.max))
            p.serialize("vector")
            p.run("vector", lambda E: E.reciprocal(sc2c[:], amax2c[:]))
            p.serialize("vector")
            p.run("vector", lambda E: E.tensor_scalar(sc2c[:], sc2c[:], 127.0, None, ALU.mult))
            p.serialize("vector")
            p.run("vector", lambda E: E.reciprocal(inv2c[:], sc2c[:]))
            p.serialize("vector")
            vC2 = p.inc("vector", lambda E: E.tensor_scalar(
                a2cols[:], inv2c[:], wts["w2si"][:], -1.0, ALU.mult, ALU.mult), "v", 1)
            p.serialize("vector")
            p.wait("tensor", "v", vC2)
            for mt in range(NMT):
                p.run("tensor", lambda E, m=mt: E.matmul(
                    pB[0:1, m * 128:(m + 1) * 128], sc2c[:, m:m + 1], id32[:],
                    start=(m % 4 == 0), stop=(m % 4 == 3)))
            tSR = p.mark("tensor", "t")
            p.wait("vector", "t", tSR)
            r2hi = _ts(nc, es_row.enter_context(nc.sbuf_tensor(
                "r2hi", [1, M], BF16, side="right")), "r2hi")
            r2lo = _ts(nc, es_row.enter_context(nc.sbuf_tensor(
                "r2lo", [1, M], BF16, side="right")), "r2lo")
            if sc2row is not None:
                p.inc("vector", lambda E: E.tensor_copy(sc2row[:], pB[0:1, :]), "v", 1)
            p.run("vector", lambda E: E.tensor_copy(r2hi[:], pB[0:1, :]))
            p.serialize("vector")
            vHL2 = p.inc("vector", lambda E: E.tensor_tensor(
                r2lo[:], pB[0:1, :], r2hi[:], ALU.subtract), "v", 1)
            p.serialize("vector")
            p.wait("tensor", "v", vHL2)
            for seg in range(4):
                p.run("tensor", lambda E, s_=seg: E.matmul(
                    pA[:, s_ * 512:(s_ + 1) * 512], ones1rb[:],
                    r2hi[0:1, s_ * 512:(s_ + 1) * 512], start=True, stop=False))
                p.run("tensor", lambda E, s_=seg: E.matmul(
                    pA[:, s_ * 512:(s_ + 1) * 512], ones1rb[:],
                    r2lo[0:1, s_ * 512:(s_ + 1) * 512], start=False, stop=True))
            tSB = p.mark("tensor", "t")
            p.wait("vector", "t", tSB)
            p.run("vector", lambda E: E.tensor_scalar(sc1B[:], pA[:, :], w1det[:], None, ALU.mult))
            vSC = p.inc("vector", lambda E: E.tensor_scalar(
                sc2B[:], pA[:, :], w2det[:], None, ALU.mult), "v", 1)
            p.wait("tensor", "v", vSC)
            es_row.close()

            t_h2q = tap("h2q", (DH, M), BF16)
            t_h2all = tap("h2all", (DH, M))
            t_raall = tap("raall", (DH, M))
            t_h2l0 = tap("h2l0", (128, 512))
            t_u1 = tap("u1d", (128, 512))
            t_sc1B = tap("sc1B", (128, M))
            t_sc2c = tap("sc2c", (128, NMT))
            t_sc2r = tap("sc2r", (1, M))
            if t_sc2c is not None:
                p.wait(STOREQ, "v", vSC)
                p.inc(STOREQ, lambda E, t=t_sc2c: E.dma_start(out=t[:], in_=sc2c[:]), "stD", 16)
            if t_sc2r is not None:
                p.wait(STOREQ, "v", vSC)
                p.inc(STOREQ, lambda E, t=t_sc2r: E.dma_start(out=t[:], in_=sc2row[:]), "stD", 16)
            t_am2 = tap("am2", (128, NMT))
            if t_sc1B is not None:
                p.wait(STOREQ, "v", vSC)
                p.inc(STOREQ, lambda E, t=t_sc1B: E.dma_start(out=t[:], in_=sc1B[:]), "stD", 16)
            if t_am2 is not None:
                p.wait(STOREQ, "v", vSC)
                p.inc(STOREQ, lambda E, t=t_am2: E.dma_start(out=t[:], in_=amax2c[:]), "stD", 16)
            last_stY = [0, 0]
            last_q4 = [0, 0, 0, 0]
            last_sME = [0, 0]
            for mb in range(4):
                mlo = mb * 512
                p.wait("vector", "t")
                p.wait("vector", "s", last_sME[0])
                p.wait("vector", "s", last_sME[1])
                if t_h2q is not None:
                    p.wait("vector", "stD")
                pend = None
                for cc in range(NCC):
                    b4 = cc % 4
                    p2 = cc % 2
                    p.wait(SYNCQ, "v", last_q4[b4])
                    if t_h2all is not None:
                        p.wait(SYNCQ, "stD")
                    sH = p.inc(SYNCQ, lambda E, b=h2l[b4], c=cc, m=mlo: E.dma_start(
                        out=b[:], in_=h2s_scr[c * 128:(c + 1) * 128, m:m + 512]), "sH2", 16)
                    if t_h2all is not None:
                        p.wait(STOREQ, "sH2", sH)
                        p.inc(STOREQ, lambda E, t=t_h2all, b=h2l[b4], c=cc, m=mlo:
                              E.dma_start(out=t[c * 128:(c + 1) * 128, m:m + 512],
                                          in_=b[:]), "stD", 16)
                    p.wait("vector", "sH2", sH)
                    if cc == 0:
                        p.wait("vector", "v", vSC)
                    p.run("vector", lambda E, b=h2l[b4], m=mlo, q=p2: E.tensor_mul(
                        u1v[:, q, :], b[:], sc1B[:, m:m + 512]))
                    vU = p.inc("vector", lambda E, b=h2l[b4], m=mlo, q=p2: E.tensor_mul(
                        u2[:, q, :], b[:], sc2B[:, m:m + 512]), "v", 1)
                    last_q4[b4] = vU
                    p.wait("scalar", "v")
                    p.run("scalar", lambda E, q=p2: E.activation(
                        ra_[:, q, :], u1v[:, q, :], AF.Identity, bias=cst["c23c"][:]))
                    p.run("scalar", lambda E, q=p2: E.activation(
                        ra_[:, q, :], ra_[:, q, :], AF.Identity, bias=cst["n23c"][:]))
                    p.run("scalar", lambda E, q=p2: E.activation(
                        rg_[:, q, :], u2[:, q, :], AF.Identity, bias=cst["c52c"][:]))
                    sQ = p.inc("scalar", lambda E, q=p2: E.activation(
                        rg_[:, q, :], rg_[:, q, :], AF.Identity, bias=cst["n52c"][:]),
                        "s", 1)
                    if pend is not None:
                        pc, psQ = pend
                        p.wait("vector", "s", psQ)
                        p.inc("vector", lambda E, c=pc, q=pc % 2: E.tensor_add(
                            h2q[:, c, :], ra_[:, q, :], rg_[:, q, :]), "v", 1)
                    pend = (cc, sQ)
                pc, psQ = pend
                p.wait("vector", "s", psQ)
                p.inc("vector", lambda E, c=pc, q=pc % 2: E.tensor_add(
                    h2q[:, c, :], ra_[:, q, :], rg_[:, q, :]), "v", 1)
                vQ = p.val("v")
                if t_h2q is not None:
                    p.wait(STOREQ, "v", vQ)
                    for cc in range(NCC):
                        p.inc(STOREQ, lambda E, c=cc, m=mlo, t=t_h2q: E.dma_start(
                            out=t[c * 128:(c + 1) * 128, m:m + 512], in_=h2q[:, c, :]), "stD", 16)
                for mi in range(4):
                    mt = 4 * mb + mi
                    par = mt % 2
                    ps = pA[:, 0:1024] if par == 0 else pA[:, 1024:2048]
                    p.wait(SYNCQ, "v")
                    sXR = p.inc(SYNCQ, lambda E, b=xre[par], m=mt: E.dma_start(
                        out=b, in_=x_d[m * 128:(m + 1) * 128, :]), "sXR", 16)
                    p.wait("tensor", "v", vQ)
                    p.wait("tensor", "s", last_sME[par])
                    for cc in range(NCC):
                        for seg in range(2):
                            p.run("tensor", lambda E, c=cc, s_=seg, mi_=mi, pp=ps,
                                  first=(cc == 0), last=(cc == NCC - 1):
                                  E.matmul(pp[:, s_ * 512:(s_ + 1) * 512],
                                           h2q[:, c, mi_ * 128:(mi_ + 1) * 128],
                                           w2qv[:, c, s_ * 512:(s_ + 1) * 512],
                                           start=first, stop=last))
                    tMM = p.mark("tensor", "t")
                    p.wait("scalar", "t", tMM)
                    p.wait("scalar", "v")
                    sME = p.inc("scalar", lambda E, pp=ps, m=mt: E.activation(
                        t1m[:], pp[:], AF.Identity, scale=a2cols[:, m:m + 1]), "s", 1)
                    last_sME[par] = sME
                    p.wait("vector", "s", sME)
                    p.wait("vector", "sXR", sXR)
                    p.wait("vector", "stY", last_stY[par])
                    p.run("vector", lambda E, b=xre[par]: E.tensor_add(t2m[:], t1m[:], b))
                    vY = p.inc("vector", lambda E, m=mt, o=ysb[par]: E.tensor_add(
                        o[:], t2m[:], attn_acc[:, m, :]), "v", 1)
                    p.wait(STOREQ, "v", vY)
                    last_stY[par] = p.inc(STOREQ, lambda E, o=ysb[par], m=mt:
                                          E.dma_start(out=y_d[m * 128:(m + 1) * 128, :],
                                                      in_=o[:]), "stY", 16)
            t_h2fin = tap("h2fin", (DH, M))
            if t_h2fin is not None:
                p.wait(SYNCQ, "v")
                p.wait(SYNCQ, "stY")
                for c in range(NCC):
                    p.inc(SYNCQ, lambda E, cc_=c, t=t_h2fin: E.dma_start(
                        out=t[cc_ * 128:(cc_ + 1) * 128, :],
                        in_=h2s_scr[cc_ * 128:(cc_ + 1) * 128, :]), "stD", 16)
            p.barrier()
    else:
        yz = sb("yz", (128, D))
        p.run("vector", lambda E: E.memset(yz[:], 0.0))
        vyz = p.mark("vector", "v")
        p.wait(STOREQ, "v", vyz)
        for mt in range(NMT):
            p.inc(STOREQ, lambda E, m=mt: E.dma_start(
                out=y_d[m * 128:(m + 1) * 128, :], in_=yz[:]), "stY", 16)

    p.wait(STOREQ, "stY", p.val("stY"))
    p.wait(STOREQ, "stD", p.val("stD"))

    def _replay(E, stream):
        for item in stream:
            fn, updates = item
            inst = fn(E)
            if updates:
                for u in updates:
                    inst = inst.then_inc(u[0], u[1])

    with nc.Block() as block:
        @block.sync
        def _(E):
            _replay(E, p.streams["sync"])

        @block.tensor
        def _(E):
            _replay(E, p.streams["tensor"])

        @block.vector
        def _(E):
            _replay(E, p.streams["vector"])

        @block.scalar
        def _(E):
            _replay(E, p.streams["scalar"])

        @block.gpsimd
        def _(E):
            _replay(E, p.streams["gpsimd"])

    es.close()
    return nc


_PI = np.concatenate([np.arange(0, D, 2), np.arange(1, D, 2)])


def _prep_inputs(inputs):
    ca = np.ascontiguousarray
    wq, wk, wv = (np.asarray(inputs[k]) for k in
                  ("L0_attn_wq", "L0_attn_wk", "L0_attn_wv"))
    ipw = np.asarray(inputs["L0_attn_in_proj_w"])
    ipb = np.asarray(inputs["L0_attn_in_proj_b"])
    wqi, wki, wvi = ipw[0:D], ipw[D:2 * D], ipw[2 * D:3 * D]
    bqi, bki, bvi = ipb[0:D], ipb[D:2 * D], ipb[2 * D:3 * D]
    cosT, sinT = _host_tables()
    maskT, id32, identbf, ones11, sel2 = _host_consts()
    common = {
        "wqT": ca(wq[_PI, :].T), "wkT": ca(wk[_PI, :].T), "wvT": ca(wv[_PI, :].T),
        "wqiT": ca(wqi[:, _PI].T), "wkiT": ca(wki[:, _PI].T), "wviT": ca(wvi[:, _PI].T),
        "d_bqi": ca(bqi.reshape(NDC, 128).T), "d_bki": ca(bki.reshape(NDC, 128).T),
        "d_bvi": ca(bvi),
        "owT": ca(np.asarray(inputs["L0_attn_out_w"]).T),
        "d_ob": ca(np.asarray(inputs["L0_attn_out_b"])),
        "w1T": ca(np.asarray(inputs["L0_ffn_w1"]).T),
        "w2T": ca(np.asarray(inputs["L0_ffn_w2"]).T),
        "d_convw": ca(np.asarray(inputs["L0_ffn_conv_w"])[:, 0, :]
                      .reshape(NCC, 128, 3).transpose(1, 2, 0).reshape(128, 3 * NCC)),
        "d_convb": ca(np.asarray(inputs["L0_ffn_conv_b"]).reshape(NCC, 128).T),
        "ropec": cosT, "ropes": sinT,
        "d_maskT": maskT, "d_id32": id32, "d_idbf": identbf,
        "d_ones11": ones11, "d_sel2": sel2,
        "d_ones1r": np.ones((1, 128), dtype=np.float32),
        "d_ones1rb": np.ones((1, 128), dtype=ml_dtypes.bfloat16),
        "d_onescol": np.ones((128, 1), dtype=np.float32),
    }
    x = np.asarray(inputs["L0_attn_x"])
    return [dict(common, x=ca(x[b])) for b in range(8)]


def kernel(**inputs):
    in_maps = _prep_inputs(inputs)
    nc = build_nc(DEBUG_TAPS, PHASES)
    res = run_bass_kernel_spmd(nc, in_maps, core_ids=list(range(8)))
    out = np.stack([np.asarray(res.results[b]["y"]) for b in range(8)], axis=0)
    if DEBUG_TAPS:
        kernel.taps = [{k: np.asarray(v) for k, v in r.items() if k.startswith("tap_")}
                       for r in res.results]
    return out


# version 14
# speedup vs baseline: 1.0212x; 1.0212x over previous
import numpy as np
import ml_dtypes
from contextlib import ExitStack

import concourse.bass as bass
import concourse.mybir as mybir
from concourse.bass_utils import run_bass_kernel_spmd
from concourse.bass_isa import ReduceOp

F32 = mybir.dt.float32
BF16 = mybir.dt.bfloat16
ALU = mybir.AluOpType
AF = mybir.ActivationFunctionType
AXX = mybir.AxisListType.X

C23 = float(1.5 * 2**23)
C52 = float(1.5 * 2**52)

M = 2048
D = 1024
H = 16
DH = 4096
NMT = M // 128
NDC = D // 128
NCC = DH // 128
NEGMASK = -1.0e9

PHASES = 4
DEBUG_TAPS = ()
TRACE_ALLOC = False


def _ts(nc, guard, name):
    if TRACE_ALLOC:
        print(f"  alloc {name}: base={nc.sbuf_base} top={nc.sbuf_top} "
              f"free={nc.sbuf_top - nc.sbuf_base}")
    return guard


def _host_tables():
    i = np.arange(D // 2, dtype=np.float32)
    theta = (np.float32(10000.0) ** ((np.float32(-2.0) * (i - np.float32(1.0)))
                                     / np.float32(D))).astype(np.float32)
    ang = (np.arange(M, dtype=np.float32)[:, None] * theta[None, :]).astype(np.float32)
    cosT = np.ascontiguousarray(np.cos(ang).astype(np.float32).T)
    sinT = np.ascontiguousarray(np.sin(ang).astype(np.float32).T)
    return cosT, sinT


def _host_consts():
    k_idx = np.arange(128)
    maskT = np.where(k_idx[:, None] <= k_idx[None, :], np.float32(0.0),
                     np.float32(NEGMASK)).astype(np.float32)
    ident32 = np.eye(128, dtype=np.float32)
    identbf = np.eye(128, dtype=ml_dtypes.bfloat16)
    ones11 = np.ones((1, 1), dtype=np.float32)
    sel2 = np.zeros((2, 128), dtype=np.float32)
    sel2[0, :64] = 1.0
    sel2[1, 64:] = 1.0
    return maskT, ident32, identbf, ones11, sel2


class Prog:
    ENGS = ("sync", "tensor", "vector", "scalar", "gpsimd")
    OWNER = {"v": "vector", "s": "scalar", "t": "tensor", "g": "gpsimd"}

    def __init__(self, nc, es):
        self.nc = nc
        self.es = es
        self.streams = {e: [] for e in self.ENGS}
        self.sems = {}
        self.cnt = {}
        self.last_real = {e: -1 for e in self.ENGS}

    def sem(self, name):
        if name not in self.sems:
            self.sems[name] = self.es.enter_context(self.nc.semaphore(name))
            self.cnt[name] = 0
        return self.sems[name]

    def run(self, eng, build):
        self.streams[eng].append((build, []))
        self.last_real[eng] = len(self.streams[eng]) - 1

    def inc(self, eng, build, sem_name, amt):
        s = self.sem(sem_name)
        self.cnt[sem_name] += amt
        val = self.cnt[sem_name]
        self.streams[eng].append((build, [[s, amt, s, val]]))
        self.last_real[eng] = len(self.streams[eng]) - 1
        return val

    def mark(self, eng, sem_name, amt=1):
        s = self.sem(sem_name)
        self.cnt[sem_name] += amt
        val = self.cnt[sem_name]
        i = self.last_real[eng]
        build, updates = self.streams[eng][i]
        for u in updates:
            if u[0] is s:
                u[1] += amt
                u[3] = val
                break
        else:
            if updates:
                ps, pv = updates[0][2], updates[0][3]
                self.streams[eng].append(
                    (lambda E, a_=ps, b_=pv: E.wait_ge(a_, b_), [[s, amt, s, val]]))
            else:
                updates.append([s, amt, s, val])
        return val

    def wait(self, eng, sem_name, val=None):
        if val is None:
            val = self.cnt.get(sem_name, 0)
        if val <= 0:
            return
        if self.OWNER.get(sem_name) == eng:
            return
        s = self.sem(sem_name)
        self.streams[eng].append((lambda E, ss=s, v=val: E.wait_ge(ss, v), None))

    def val(self, sem_name):
        self.sem(sem_name)
        return self.cnt[sem_name]

    def serialize(self, eng):
        i = self.last_real[eng]
        build, updates = self.streams[eng][i]
        if updates:
            ss, val = updates[0][2], updates[0][3]
        else:
            name = "ch_" + eng
            ss = self.sem(name)
            self.cnt[name] += 1
            val = self.cnt[name]
            updates.append([ss, 1, ss, val])
        self.streams[eng].append((lambda E, s_=ss, v=val: E.wait_ge(s_, v), None))
        return val

    def barrier(self):
        snapshot = dict(self.cnt)
        for eng in self.ENGS:
            for name, v in snapshot.items():
                self.wait(eng, name, v)


def build_nc(debug_taps=(), phases=PHASES):
    nc = bass.Bass("TRN2", target_bir_lowering=False,
                   detect_race_conditions=False)
    es = ExitStack()
    p = Prog(nc, es)

    def din(name, shape, dt=F32):
        return nc.dram_tensor(name, list(shape), dt, kind="ExternalInput").ap()

    x_d = din("x", (M, D))
    wT_d = {"q": din("wqT", (D, D)), "k": din("wkT", (D, D)), "v": din("wvT", (D, D))}
    wiT_d = {"q": din("wqiT", (D, D)), "k": din("wkiT", (D, D)), "v": din("wviT", (D, D))}
    bqi_d = din("d_bqi", (128, NDC)); bki_d = din("d_bki", (128, NDC)); bvi_d = din("d_bvi", (D,))
    owT_d = din("owT", (D, D)); ob_d = din("d_ob", (D,))
    w1T_d = din("w1T", (D, DH)); w2T_d = din("w2T", (DH, D))
    convw_d = din("d_convw", (128, 3 * NCC)); convb_d = din("d_convb", (128, NCC))
    ropec_d = din("ropec", (D // 2, M)); ropes_d = din("ropes", (D // 2, M))
    maskT_d = din("d_maskT", (128, 128))
    id32_d = din("d_id32", (128, 128)); idbf_d = din("d_idbf", (128, 128), BF16)
    ones11_d = din("d_ones11", (1, 1)); sel2_d = din("d_sel2", (2, 128))
    ones1r_d = din("d_ones1r", (1, 128)); onescol_d = din("d_onescol", (128, 1))
    ones1rb_d = din("d_ones1rb", (1, 128), BF16)

    y_d = nc.dram_tensor("y", [M, D], F32, kind="ExternalOutput").ap()
    xqT_scr = nc.dram_tensor("xqT_scr", [D, M], BF16, kind="Internal").ap()
    h2s_scr = nc.dram_tensor("h2s_scr", [DH, M], F32, kind="Internal").ap()

    taps = {}
    def tap(name, shape, dt=F32):
        if name in debug_taps:
            taps[name] = nc.dram_tensor("tap_" + name, list(shape), dt,
                                        kind="ExternalOutput").ap()
            return taps[name]
        return None

    def sb(name, shape, dt=F32):
        g = es.enter_context(nc.sbuf_tensor(name, list(shape), dt))
        return _ts(nc, g, name)

    maskT = sb("maskT", (128, 128))
    id32 = sb("id32", (128, 128))
    idbf = sb("idbf", (128, 128), BF16)
    ones1r = sb("ones1r", (1, 128))
    ones1rb = sb("ones1rb", (1, 128), BF16)
    ones_col = sb("ones_col", (128, 1))
    bqik = {"q": sb("bqi", (128, NDC)), "k": sb("bki", (128, NDC))}
    bviB = sb("bviB", (128, D), BF16)
    obB = sb("obB", (128, D), BF16)
    convw = sb("convw", (128, 3 * NCC))
    convb = sb("convb", (128, NCC))
    halfc = sb("halfc", (128, 1))
    ddet = sb("ddet", (128, 1))
    w1det = sb("w1det", (128, 1))
    w2det = sb("w2det", (128, 1))
    cst = {}
    inv1_row = sb("inv1_row", (1, M))
    wts = {}
    for nmm in ("w1s", "w1si", "w2s", "w2si"):
        wts[nmm] = sb(nmm, (128, 1))

    pA = es.enter_context(nc.psum_tensor("pA", [128, 2048], F32))
    pB = es.enter_context(nc.psum_tensor("pB", [128, 2048], F32))

    SYNCQ = "sync"
    STOREQ = "scalar"
    CASTQ = "gpsimd"

    for (dst, src) in ((maskT, maskT_d), (id32, id32_d), (idbf, idbf_d),
                       (ones1r, ones1r_d), (ones_col, onescol_d),
                       (ones1rb, ones1rb_d)):
        p.inc(SYNCQ, lambda E, d=dst, s=src: E.dma_start(out=d[:], in_=s[:]), "sLC", 16)
    p.inc(SYNCQ, lambda E: E.dma_start(out=bqik["q"][:], in_=bqi_d[:]), "sLC", 16)
    p.inc(SYNCQ, lambda E: E.dma_start(out=bqik["k"][:], in_=bki_d[:]), "sLC", 16)
    p.inc(CASTQ, lambda E: E.dma_start(out=bviB[0:1, :], in_=bvi_d[None, :]), "gB", 16)
    gB = p.inc(CASTQ, lambda E: E.dma_start(out=obB[0:1, :], in_=ob_d[None, :]), "gB", 16)
    p.inc(SYNCQ, lambda E: E.dma_start(out=convw[:], in_=convw_d[:]), "sLC", 16)
    p.inc(SYNCQ, lambda E: E.dma_start(out=convb[:], in_=convb_d[:]), "sLC", 16)
    cLC = p.val("sLC")

    p.run("vector", lambda E: E.memset(halfc[:], 0.5))
    p.serialize("vector")
    p.run("vector", lambda E: E.tensor_scalar(ddet[:], halfc[:], C23, -C23, ALU.add, ALU.add))
    p.serialize("vector")
    p.run("vector", lambda E: E.tensor_scalar(w2det[:], ddet[:], 2.0, None, ALU.mult))
    p.serialize("vector")
    p.inc("vector", lambda E: E.tensor_scalar(w1det[:], ddet[:], -2.0, 1.0,
                                              ALU.mult, ALU.add), "v", 1)
    p.serialize("vector")
    for nm_, vv_ in (("c23c", C23), ("n23c", -C23), ("c52c", C52), ("n52c", -C52)):
        cst[nm_] = sb(nm_, (128, 1))
        p.run("vector", lambda E, tt=cst[nm_], v=vv_: E.memset(tt[:], v))
    p.mark("vector", "v")
    p.wait("tensor", "sLC", cLC)
    p.wait("tensor", "gB", gB)
    for bi, bt in enumerate((bviB, obB)):
        for seg in range(2):
            p.run("tensor", lambda E, b=bt, s_=seg, i=bi: E.matmul(
                pA[:, i * 1024 + s_ * 512:i * 1024 + (s_ + 1) * 512],
                ones1rb[:], b[0:1, s_ * 512:(s_ + 1) * 512],
                start=True, stop=True))
    tBC = p.mark("tensor", "t")
    p.wait("vector", "t", tBC)
    for bi, bt in enumerate((bviB, obB)):
        p.run("vector", lambda E, b=bt, i=bi: E.tensor_copy(
            b[:], pA[:, i * 1024:(i + 1) * 1024]))
    vBC = p.mark("vector", "v")
    p.wait("tensor", "v", vBC)

    es0b = ExitStack()
    sb0b = lambda name, shape, dt=F32: _ts(nc, es0b.enter_context(
        nc.sbuf_tensor(name, list(shape), dt)), name)
    wmb = [sb0b(f"wmb{i}", (128, DH)) for i in range(2)]
    wab = [sb0b(f"wab{i}", (128, DH)) for i in range(2)]
    wabs = {"w1": sb0b("wabs1", (128, 1)), "w2": sb0b("wabs2", (128, 1))}
    wtmp = sb0b("wtmp", (128, 1))
    for key, dram, nch in (("w1", w1T_d, NDC), ("w2", w2T_d, NCC)):
        w = dram.shape[1]
        for ch in range(nch):
            buf = wmb[ch % 2]
            abuf = wab[ch % 2]
            p.wait(SYNCQ, "s")
            p.inc(SYNCQ, lambda E, b=buf, d=dram, c=ch, ww=w: E.dma_start(
                out=b[:, 0:ww], in_=d[c * 128:(c + 1) * 128, :]), "sLW", 16)
            p.wait("scalar", "sLW")
            p.wait("scalar", "v")
            p.inc("scalar", lambda E, b=buf, a=abuf, ww=w: E.activation(
                a[:, 0:ww], b[:, 0:ww], AF.Abs), "s", 1)
            p.wait("vector", "s")
            if ch == 0:
                p.inc("vector", lambda E, a=wabs[key], ab=abuf, ww=w: E.reduce_sum(
                    a[:], ab[:, 0:ww], AXX), "v", 1)
            else:
                p.inc("vector", lambda E, ab=abuf, ww=w: E.reduce_sum(
                    wtmp[:], ab[:, 0:ww], AXX), "v", 1)
                p.serialize("vector")
                p.inc("vector", lambda E, a=wabs[key]: E.tensor_add(a[:], a[:], wtmp[:]),
                      "v", 1)
                p.serialize("vector")
    vWR = p.val("v")
    p.wait("tensor", "v", vWR)
    p.run("tensor", lambda E: E.matmul(pB[0:1, 0:1], ones_col[:], wabs["w1"][:],
                                       start=True, stop=True))
    tWR = p.inc("tensor", lambda E: E.matmul(pB[0:1, 512:513], ones_col[:],
                                             wabs["w2"][:], start=True, stop=True),
                "t", 1)
    wsum = {"w1": wabs["w1"], "w2": wabs["w2"]}
    p.wait("vector", "t", tWR)
    p.run("vector", lambda E: E.tensor_copy(wabs["w1"][0:1, :], pB[0:1, 0:1]))
    p.serialize("vector")
    p.run("vector", lambda E: E.tensor_copy(wabs["w2"][0:1, :], pB[0:1, 512:513]))
    p.serialize("vector")
    for key in ("w1", "w2"):
        p.run("vector", lambda E, a=wabs[key]: E.tensor_scalar(
            a[0:1, :], a[0:1, :], 1.0 / (D * DH), 1e-5, ALU.mult, ALU.max))
        p.serialize("vector")
        p.run("vector", lambda E, a=wabs[key]: E.reciprocal(a[0:1, :], a[0:1, :]))
        p.serialize("vector")
    vWS = p.mark("vector", "v")
    p.wait("tensor", "v", vWS)
    p.run("tensor", lambda E: E.matmul(pB[0:128, 1024:1025], ones1r[:],
                                       wabs["w1"][0:1, :], start=True, stop=True))
    tWS = p.inc("tensor", lambda E: E.matmul(pB[0:128, 1536:1537], ones1r[:],
                                             wabs["w2"][0:1, :], start=True, stop=True),
                "t", 1)
    p.wait("vector", "t", tWS)
    p.run("vector", lambda E: E.tensor_copy(wts["w1s"][:], pB[:, 1024:1025]))
    p.serialize("vector")
    p.run("vector", lambda E: E.tensor_copy(wts["w2s"][:], pB[:, 1536:1537]))
    p.serialize("vector")
    p.run("vector", lambda E: E.reciprocal(wts["w1si"][:], wts["w1s"][:]))
    p.serialize("vector")
    p.run("vector", lambda E: E.reciprocal(wts["w2si"][:], wts["w2s"][:]))
    p.serialize("vector")
    vWSS = p.mark("vector", "v")
    p.wait("tensor", "v", vWSS)
    es0b.close()

    es_xnT = ExitStack()
    xnT = [es_xnT.enter_context(nc.sbuf_tensor(f"xnT{c}", [128, M], BF16, side="right"))
           for c in range(NDC)]
    with ExitStack() as es1:
        sb1 = lambda name, shape, dt=F32: es1.enter_context(
            nc.sbuf_tensor(name, list(shape), dt))
        xall = [sb1(f"xall{t}", (128, D)) for t in range(NMT)]
        xq32 = [sb1(f"xq32_{i}", (128, D)) for i in range(2)]
        xqTst = [sb1(f"xqTst{c}", (128, M), BF16) for c in range(NDC)]
        sqd = [sb1(f"sqd{i}", (128, D)) for i in range(2)]
        mvA = sb1("mvA", (128, NMT)); mvB = sb1("mvB", (128, NMT))
        s0 = sb1("s0", (128, NMT)); r0 = sb1("r0", (128, NMT)); r1 = sb1("r1", (128, NMT))
        va = sb1("va", (128, NMT)); tN = sb1("tN", (128, NMT)); nmb = sb1("nmb", (128, NMT))
        amax = sb1("amax", (128, NMT)); rc_ = sb1("rc", (128, NMT))
        sc1 = sb1("sc1", (128, NMT)); iv1 = sb1("iv1", (128, NMT))
        sw1 = sb1("sw1", (128, NMT)); sw2 = sb1("sw2", (128, NMT))
        rta = sb1("rta", (128, D)); rtb = sb1("rtb", (128, D)); rth = sb1("rth", (128, D))

        t_xn32 = tap("xn32", (M, D))
        t_xqbf = tap("xqbf", (M, D))
        t_inv1 = tap("inv1", (1, M))

        sXs = []
        for mt in range(NMT):
            sXs.append(p.inc(SYNCQ, lambda E, b=xall[mt], m=mt: E.dma_start(
                out=b[:], in_=x_d[m * 128:(m + 1) * 128, :]), "sX", 16))
        vRed = [0] * NMT
        for mt in range(NMT):
            par = mt % 2
            p.wait("scalar", "sX", sXs[mt])
            if mt >= 2:
                p.wait("scalar", "v", vRed[mt - 2])
            sD = p.inc("scalar", lambda E, b=xall[mt], q=sqd[par]: E.activation(
                q[:], b[:], AF.Square), "s", 1)
            p.wait("vector", "sX", sXs[mt])
            p.run("vector", lambda E, b=xall[mt], m=mt: E.reduce_sum(
                mvA[:, m:m + 1], b[:], AXX))
            p.wait("vector", "s", sD)
            vRed[mt] = p.inc("vector", lambda E, q=sqd[par], m=mt: E.reduce_sum(
                mvB[:, m:m + 1], q[:], AXX), "v", 1)
        p.serialize("vector")
        p.run("vector", lambda E: E.tensor_scalar(mvA[:], mvA[:], 1.0 / D, None, ALU.mult))
        p.serialize("vector")
        p.run("vector", lambda E: E.tensor_scalar(mvB[:], mvB[:], 1.0 / D, None, ALU.mult))
        p.serialize("vector")
        p.run("vector", lambda E: E.tensor_mul(va[:], mvA[:], mvA[:]))
        p.serialize("vector")
        p.run("vector", lambda E: E.tensor_tensor(va[:], mvB[:], va[:], ALU.subtract))
        p.serialize("vector")
        vVA = p.inc("vector", lambda E: E.tensor_scalar(
            va[:], va[:], 1e-5, None, ALU.add), "v", 1)
        p.serialize("vector")
        p.wait("scalar", "v", vVA)
        sSQ = p.inc("scalar", lambda E: E.activation(s0[:], va[:], AF.Sqrt), "s", 1)
        p.wait("vector", "s", sSQ)
        p.run("vector", lambda E: E.reciprocal(r0[:], s0[:]))
        p.serialize("vector")
        p.run("vector", lambda E: E.tensor_mul(tN[:], r0[:], r0[:]))
        p.serialize("vector")
        p.run("vector", lambda E: E.tensor_mul(tN[:], tN[:], va[:]))
        p.serialize("vector")
        p.run("vector", lambda E: E.tensor_scalar(tN[:], tN[:], -0.5, 1.5, ALU.mult, ALU.add))
        p.serialize("vector")
        p.run("vector", lambda E: E.tensor_mul(r1[:], r0[:], tN[:]))
        p.serialize("vector")
        p.run("vector", lambda E: E.tensor_scalar(nmb[:], mvA[:], -1.0, None, ALU.mult))
        p.serialize("vector")
        vRS = p.inc("vector", lambda E: E.tensor_mul(nmb[:], nmb[:], r1[:]), "v", 1)
        p.serialize("vector")
        p.wait("scalar", "v", vRS)
        sXN = [0] * NMT
        vMX = [0] * NMT
        for mt in range(NMT):
            par = mt % 2
            sXN[mt] = p.inc("scalar", lambda E, b=xall[mt], m=mt: E.activation(
                b[:], b[:], AF.Identity, bias=nmb[:, m:m + 1],
                scale=r1[:, m:m + 1]), "s", 1)
            if t_xn32 is not None:
                p.wait(STOREQ, "s", sXN[mt])
                p.inc(STOREQ, lambda E, i=xall[mt], m=mt: E.dma_start(
                    out=t_xn32[m * 128:(m + 1) * 128, :], in_=i[:]), "stD", 16)
            if mt >= 2:
                p.wait("scalar", "v", vMX[mt - 2])
            sAB = p.inc("scalar", lambda E, i=xall[mt], q=sqd[par]: E.activation(
                q[:], i[:], AF.Abs), "s", 1)
            p.wait("vector", "s", sAB)
            vMX[mt] = p.inc("vector", lambda E, q=sqd[par], m=mt: E.reduce_max(
                amax[:, m:m + 1], q[:], AXX), "v", 1)
            p.wait("tensor", "s", sXN[mt])
            if mt >= 1:
                p.wait("tensor", "v", vXnEv[mt - 1])
            for dc in range(NDC):
                p.run("tensor", lambda E, s=xall[mt], d=dc: E.transpose(
                    pA[:, d * 128:(d + 1) * 128], s[:, d * 128:(d + 1) * 128],
                    id32[:]))
            tT = p.mark("tensor", "t")
            p.wait("vector", "t", tT)
            for dc in range(NDC):
                p.run("vector", lambda E, d=dc, m=mt:
                      E.tensor_copy(xnT[d][:, m * 128:(m + 1) * 128],
                                    pA[:, d * 128:(d + 1) * 128]))
            if mt == 0:
                vXnEv = [0] * NMT
            vXnEv[mt] = p.mark("vector", "v")
        p.serialize("vector")
        p.run("vector", lambda E: E.tensor_scalar(amax[:], amax[:], 1e-5, None, ALU.max))
        p.serialize("vector")
        p.run("vector", lambda E: E.reciprocal(rc_[:], amax[:]))
        p.serialize("vector")
        p.run("vector", lambda E: E.tensor_scalar(sc1[:], rc_[:], 127.0, None, ALU.mult))
        p.serialize("vector")
        p.run("vector", lambda E: E.reciprocal(iv1[:], sc1[:]))
        p.serialize("vector")
        p.run("vector", lambda E: E.tensor_scalar(sw1[:], sc1[:], w1det[:], None, ALU.mult))
        p.serialize("vector")
        p.run("vector", lambda E: E.tensor_scalar(sw2[:], sc1[:], w2det[:], None, ALU.mult))
        p.serialize("vector")
        vXqEv = [0] * NMT
        for mt in range(NMT):
            par = mt % 2
            if mt >= 2:
                p.wait("vector", "t", tXq[mt - 2] if mt >= 2 else 0)
            p.run("vector", lambda E, i=xall[mt], m=mt: E.tensor_scalar(
                rta[:], i[:], sw1[:, m:m + 1], C23, ALU.mult, ALU.add))
            p.run("vector", lambda E, i=xall[mt], m=mt: E.tensor_scalar(
                rtb[:], i[:], sw2[:, m:m + 1], C52, ALU.mult, ALU.add))
            p.run("vector", lambda E: E.tensor_scalar(rth[:], rta[:], -C23, None, ALU.add))
            vXQ = p.inc("vector", lambda E, o=xq32[par]: E.scalar_tensor_tensor(
                o[:], rtb[:], -C52, rth[:], ALU.add, ALU.add), "v", 1)
            if t_xqbf is not None:
                p.wait(STOREQ, "v", vXQ)
                p.inc(STOREQ, lambda E, o=xq32[par], m=mt: E.dma_start(
                    out=t_xqbf[m * 128:(m + 1) * 128, :], in_=o[:]), "stD", 16)
            p.wait("tensor", "v", vXQ)
            p.run("tensor", lambda E, m=mt: E.matmul(
                pB[0:1, m * 128:(m + 1) * 128], iv1[:, m:m + 1], id32[:],
                start=(m % 4 == 0), stop=(m % 4 == 3)))
            if mt >= 1:
                p.wait("tensor", "v", vXqEv[mt - 1])
            for dc in range(NDC):
                p.run("tensor", lambda E, s=xq32[par], d=dc: E.transpose(
                    pA[:, 1024 + d * 128:1024 + (d + 1) * 128],
                    s[:, d * 128:(d + 1) * 128], id32[:]))
            if mt == 0:
                tXq = [0] * NMT
            tXq[mt] = p.mark("tensor", "t")
            p.wait("vector", "t", tXq[mt])
            for dc in range(NDC):
                p.run("vector", lambda E, d=dc, m=mt:
                      E.tensor_copy(xqTst[d][:, m * 128:(m + 1) * 128],
                                    pA[:, 1024 + d * 128:1024 + (d + 1) * 128]))
            vXqEv[mt] = p.mark("vector", "v")
        tIV = p.mark("tensor", "t")
        p.wait("vector", "t", tIV)
        vIV = p.inc("vector", lambda E: E.tensor_copy(inv1_row[:], pB[0:1, :]), "v", 1)
        if t_inv1 is not None:
            p.wait(STOREQ, "v", vIV)
            p.inc(STOREQ, lambda E: E.dma_start(out=t_inv1[:], in_=inv1_row[:]), "stD", 16)
        p.wait(STOREQ, "v")
        for dc in range(NDC):
            p.inc(STOREQ, lambda E, d=dc: E.dma_start(
                out=xqT_scr[d * 128:(d + 1) * 128, :], in_=xqTst[d][:]), "stQ", 16)
        p.barrier()

    if "xnT" in debug_taps:
        t_xnT = tap("xnT", (D, M), BF16)
        for dc in range(NDC):
            p.inc(STOREQ, lambda E, d=dc, t=t_xnT: E.dma_start(
                out=t[d * 128:(d + 1) * 128, :], in_=xnT[d][:]), "stD", 16)

    if phases >= 2:
        attn_acc = sb("attn_acc", (128, NMT, D), BF16)
        p.es_qk = ExitStack()
        qh = [_ts(nc, p.es_qk.enter_context(nc.sbuf_tensor(f"qh{c}", [128, M], BF16)), f"qh{c}")
              for c in range(NDC)]
        kh = [_ts(nc, p.es_qk.enter_context(nc.sbuf_tensor(f"kh{c}", [128, M], BF16)), f"kh{c}")
              for c in range(NDC)]
        p.es_vh = ExitStack()
        late = {}

        with ExitStack() as es2:
            sb2 = lambda name, shape, dt=F32: _ts(nc, es2.enter_context(
                nc.sbuf_tensor(name, list(shape), dt)), name)
            es_tab = ExitStack()
            ropec = [es_tab.enter_context(nc.sbuf_tensor(
                f"ropec{c}", [128, M], BF16, side="right")) for c in range(2)]
            ropes = [es_tab.enter_context(nc.sbuf_tensor(
                f"ropes{c}", [128, M], BF16, side="right")) for c in range(2)]
            tab_loaded = {}

            def load_tables(c):
                par = c % 2
                p.wait(CASTQ, "v")
                p.inc(CASTQ, lambda E, cc=c, pp=par: E.dma_start(
                    out=ropec[pp][:], in_=ropec_d[cc * 128:(cc + 1) * 128, :]), "gT", 16)
                tab_loaded[c] = p.inc(CASTQ, lambda E, cc=c, pp=par: E.dma_start(
                    out=ropes[pp][:], in_=ropes_d[cc * 128:(cc + 1) * 128, :]), "gT", 16)
            wb2 = [sb2(f"wb2_{i}", (128, D), BF16) for i in range(2)]
            q0 = [sb2(f"q0_{i}", (128, M), BF16) for i in range(2)]
            qr = [sb2(f"qr{c}", (128, M), BF16) for c in range(NDC)]
            rt1 = sb2("rt1", (128, M), BF16)



            t_q0 = tap("q0", (D, M), BF16)
            t_qr = tap("qr", (D, M), BF16)
            t_qh = tap("qh", (D, M), BF16)
            t_kh = tap("kh", (D, M), BF16)
            t_vh = tap("vh", (128, NMT * H * 65), BF16)

            def load_wcol(dram, ic, wb, sem_name):
                return p.inc(CASTQ, lambda E, w=wb, d=dram, i=ic: E.dma_start(
                    out=w[:].rearrange("p (dc q) -> p dc q", dc=NDC),
                    in_=d.rearrange("(dc p) i -> p dc i", p=128)[:, :, i * 128:(i + 1) * 128]),
                    sem_name, 16)

            order = [0, 4, 1, 5, 2, 6, 3, 7]

            def fm_projection(wdram, moving, consume):
                p.wait("tensor", "v")
                for idx, ic in enumerate(order):
                    wb = wb2[idx % 2]
                    p.wait(CASTQ, "t")
                    gW = load_wcol(wdram, ic, wb, "gW")
                    p.wait("tensor", "gW", gW)
                    for mh in range(2):
                        ps = pA[:, mh * 1024:(mh + 1) * 1024]
                        p.wait("tensor", "s")
                        for dc in range(NDC):
                            for seg in range(2):
                                ml = mh * 1024 + seg * 512
                                p.run("tensor", lambda E, w=wb, d=dc, s_=seg, ml_=ml,
                                      pp=ps, first=(dc == 0), last=(dc == NDC - 1),
                                      mo=moving: E.matmul(
                                          pp[:, s_ * 512:(s_ + 1) * 512],
                                          w[:, d * 128:(d + 1) * 128],
                                          mo[d][:, ml_:ml_ + 512],
                                          start=first, stop=last))
                        tMM = p.mark("tensor", "t")
                        consume(ic, mh, ps, tMM)

            def run_qkv(kind):
                seen = {}
                tq0 = t_q0 if (kind == "q") else None

                def consume(ic, mh, ps, tMM):
                    par = 0 if ic < 4 else 1
                    p.wait("scalar", "t", tMM)
                    p.wait("scalar", "v")
                    sEV = p.inc("scalar", lambda E, q=q0[par], m=mh, pp=ps:
                                E.activation(q[:, m * 1024:(m + 1) * 1024], pp[:],
                                             AF.Identity), "s", 1)
                    if tq0 is not None:
                        p.wait(STOREQ, "s", sEV)
                        p.inc(STOREQ, lambda E, q=q0[par], i=ic, m=mh, t=tq0:
                              E.dma_start(
                                  out=t[i * 128:(i + 1) * 128, m * 1024:(m + 1) * 1024],
                                  in_=q[:, m * 1024:(m + 1) * 1024]), "stD", 16)
                    seen[(ic, mh)] = True
                    c = ic - 4
                    if par == 1 and mh == 1 and seen.get((c, 1)):
                        tp_ = c % 2
                        if c < 3:
                            load_tables(c + 1)
                        p.wait("vector", "s")
                        p.wait("vector", "t")
                        p.wait("vector", "gT", tab_loaded[c])
                        p.run("vector", lambda E, cc=c, tb=tp_: E.tensor_mul(qr[cc][:], q0[0][:], ropec[tb][:]))
                        p.run("vector", lambda E, tb=tp_: E.tensor_mul(rt1[:], q0[1][:], ropes[tb][:]))
                        p.run("vector", lambda E, cc=c: E.tensor_add(qr[cc][:], qr[cc][:], rt1[:]))
                        p.run("vector", lambda E, cc=c, tb=tp_: E.tensor_mul(qr[cc + 4][:], q0[1][:], ropec[tb][:]))
                        p.run("vector", lambda E, tb=tp_: E.tensor_mul(rt1[:], q0[0][:], ropes[tb][:]))
                        p.inc("vector", lambda E, cc=c: E.tensor_tensor(
                            qr[cc + 4][:], qr[cc + 4][:], rt1[:], ALU.subtract), "v", 1)

                load_tables(0)
                fm_projection(wT_d[kind], xnT, consume)
                vROPE = p.val("v")
                if kind == "q" and t_qr is not None:
                    p.wait(STOREQ, "v", vROPE)
                    for c in range(NDC):
                        p.inc(STOREQ, lambda E, cc=c, t=t_qr: E.dma_start(
                            out=t[cc * 128:(cc + 1) * 128, :], in_=qr[cc][:]), "stD", 16)

                if kind in ("q", "k"):
                    dstset = qh if kind == "q" else kh
                    bias = bqik[kind]

                    def consume2(ic, mh, ps, tMM):
                        p.wait("scalar", "t", tMM)
                        p.inc("scalar", lambda E, o=dstset[ic], m=mh, pp=ps,
                              b=bias, i=ic: E.activation(
                                  o[:, m * 1024:(m + 1) * 1024], pp[:],
                                  AF.Identity, bias=b[:, i:i + 1]), "s", 1)

                    fm_projection(wiT_d[kind], qr, consume2)
                    tp = t_qh if kind == "q" else t_kh
                    if tp is not None:
                        p.wait(STOREQ, "s")
                        for c in range(NDC):
                            p.inc(STOREQ, lambda E, cc=c, ds=dstset, t=tp: E.dma_start(
                                out=t[cc * 128:(cc + 1) * 128, :], in_=ds[cc][:]), "stD", 16)
                else:
                    es_tab.close()
                    es_xnT.close()
                    wv3 = sb2("wv3", (128, NDC * D), BF16)
                    vh_ext = p.es_vh.enter_context(
                        nc.sbuf_tensor("vh_ext", [128, NMT, H * 65], BF16, side="right"))
                    late["vh_ext"] = vh_ext
                    p.run("vector", lambda E: E.memset(vh_ext[:], 1.0))
                    vONES = p.mark("vector", "v")
                    p.wait(CASTQ, "v", vROPE)
                    gW = p.inc(CASTQ, lambda E: E.dma_start(
                        out=wv3[:].rearrange("p (dc i) -> p dc i", dc=NDC),
                        in_=wiT_d["v"].rearrange("(dc p) i -> p dc i", p=128)), "gW", 16)
                    p.wait("tensor", "gW", gW)
                    p.wait("tensor", "v", vROPE)
                    wv3v = wv3[:].rearrange("p (dc i) -> p dc i", dc=NDC)
                    for mt in range(NMT):
                        ps = pA[:, 0:1024] if mt % 2 == 0 else pA[:, 1024:2048]
                        p.wait("tensor", "v")
                        for dc in range(NDC):
                            for seg in range(2):
                                p.run("tensor", lambda E, d=dc, s_=seg, m=mt, pp=ps,
                                      first=(dc == 0), last=(dc == NDC - 1):
                                      E.matmul(pp[:, s_ * 512:(s_ + 1) * 512],
                                               qr[d][:, m * 128:(m + 1) * 128],
                                               wv3v[:, d, s_ * 512:(s_ + 1) * 512],
                                               start=first, stop=last))
                        tMM = p.mark("tensor", "t")
                        p.wait("vector", "t", tMM)
                        p.wait("vector", "v", vONES)
                        p.inc("vector", lambda E, m=mt, pp=ps: E.tensor_add(
                            vh_ext[:, m, :].rearrange("p (h q) -> p h q", q=65)[:, :, 0:64],
                            pp.rearrange("p (h q) -> p h q", q=64),
                            bviB[:].rearrange("p (h q) -> p h q", q=64)), "v", 1)

            run_qkv("q")
            run_qkv("k")
            run_qkv("v")
            vh_ext = late["vh_ext"]
            if t_vh is not None:
                p.wait(STOREQ, "v")
                p.inc(STOREQ, lambda E, t=t_vh: E.dma_start(
                    out=t[:], in_=vh_ext[:].rearrange("p a b -> p (a b)")), "stD", 16)
            p.barrier()

    if phases >= 3:
        es_att = ExitStack()
        oT = _ts(nc, es_att.enter_context(
            nc.sbuf_tensor("oT", [128, NDC, M], BF16, side="right")), "oT")
        with ExitStack() as es3:
            sb3 = lambda name, shape, dt=F32: _ts(nc, es3.enter_context(
                nc.sbuf_tensor(name, list(shape), dt)), name)
            pT0 = [sb3(f"pT0_{i}", (128, 512), BF16) for i in range(2)]
            pT1 = [sb3(f"pT1_{i}", (128, 512), BF16) for i in range(2)]
            zr0 = sb3("zr0", (1, 512)); zr1 = sb3("zr1", (1, 512))
            zb_sb = sb3("zb_sb", (128, 512))
            t_oT = tap("oT", (128, NDC * M), BF16)

            last_sEX = [0, 0]
            last_PV = [0, 0]
            vEV_prev = [0]
            for hp in range(NDC):
                h0c = 65 * (2 * hp)
                h1c = 65 * (2 * hp + 1)
                for qc in range(4):
                    nt = 4 * qc + 4
                    qbase = qc * 512
                    sEX_of = {}

                    def emit_pv(t, h0c=h0c, h1c=h1c, qbase=qbase, nt=nt):
                        par = t % 2
                        off = max(0, t * 128 - qbase)
                        N = 512 - off
                        if t == 0 and vEV_prev[0]:
                            p.wait("tensor", "v", vEV_prev[0])
                        p.wait("tensor", "s", sEX_of[t])
                        p.run("tensor", lambda E, tt=t, o=off, n=N, b=pT0[par],
                              hc=h0c, first=(t == 0), last=(t == nt - 1):
                              E.matmul(pB[0:65, o:512], vh_ext[:, tt, hc:hc + 65],
                                       b[:, 0:n], start=first, stop=last,
                                       tile_position=(0, 0)))
                        p.run("tensor", lambda E, tt=t, o=off, n=N, b=pT1[par],
                              hc=h1c, first=(t == 0), last=(t == nt - 1):
                              E.matmul(pB[64:128, 512 + o:1024],
                                       vh_ext[:, tt, hc:hc + 64],
                                       b[:, 0:n], start=first, stop=last,
                                       tile_position=(0, 64)))
                        last_PV[par] = p.inc(
                            "tensor", lambda E, tt=t, o=off, n=N, b=pT1[par],
                            hc=h1c, first=(t == 0), last=(t == nt - 1):
                            E.matmul(pB[0:1, 1024 + o:1536],
                                     vh_ext[:, tt, hc + 64:hc + 65],
                                     b[:, 0:n], start=first, stop=last,
                                     tile_position=(0, 0)), "t", 1)

                    for t in range(nt):
                        par = t % 2
                        sS0 = pA[:, par * 1024:par * 1024 + 512]
                        sS1 = pA[:, par * 1024 + 512:par * 1024 + 1024]
                        off = max(0, t * 128 - qbase)
                        N = 512 - off
                        qlo = qbase + off
                        p.wait("tensor", "s", last_sEX[par])
                        p.run("tensor", lambda E, s_=sS0, o=off, n=N, ql=qlo, tt=t,
                              h=hp: E.matmul(s_[:, o:o + n],
                                             kh[h][0:64, tt * 128:(tt + 1) * 128],
                                             qh[h][0:64, ql:ql + n],
                                             start=True, stop=True,
                                             tile_position=(0, 0)))
                        tQK = p.inc("tensor", lambda E, s_=sS1, o=off, n=N, ql=qlo,
                                    tt=t, h=hp: E.matmul(
                                        s_[:, o:o + n],
                                        kh[h][64:128, tt * 128:(tt + 1) * 128],
                                        qh[h][64:128, ql:ql + n],
                                        start=True, stop=True,
                                        tile_position=(64, 0)), "t", 1)
                        if t * 128 >= qbase:
                            p.wait("vector", "t", tQK)
                            p.run("vector", lambda E, s_=sS0, o=off:
                                  E.tensor_add(s_[:, o:o + 128], s_[:, o:o + 128], maskT[:]))
                            vMK = p.inc("vector", lambda E, s_=sS1, o=off:
                                        E.tensor_add(s_[:, o:o + 128], s_[:, o:o + 128],
                                                     maskT[:]), "v", 1)
                            p.wait("scalar", "v", vMK)
                        else:
                            p.wait("scalar", "t", tQK)
                        p.wait("scalar", "t", last_PV[par])
                        p.run("scalar", lambda E, s_=sS0, o=off, n=N, b=pT0[par]:
                              E.activation(b[:, 0:n], s_[:, o:o + n], AF.Exp, scale=0.125))
                        sEX = p.inc("scalar", lambda E, s_=sS1, o=off, n=N, b=pT1[par]:
                                    E.activation(b[:, 0:n], s_[:, o:o + n], AF.Exp,
                                                 scale=0.125), "s", 1)
                        last_sEX[par] = sEX
                        sEX_of[t] = sEX
                        if t >= 1:
                            emit_pv(t - 1)
                    emit_pv(nt - 1)
                    tPV = p.val("t")
                    p.wait("vector", "t", tPV)
                    p.run("vector", lambda E: E.reciprocal(zr0[:], pB[64:65, 0:512]))
                    vRC = p.inc("vector", lambda E: E.reciprocal(
                        zr1[:], pB[0:1, 1024:1536]), "v", 1)
                    p.wait("tensor", "v", vRC)
                    p.run("tensor", lambda E: E.matmul(
                        pB[0:64, 1536:2048], ones1r[0:1, 0:64], zr0[:],
                        start=True, stop=True))
                    tZB = p.inc("tensor", lambda E: E.matmul(
                        pB[64:128, 1536:2048], ones1r[0:1, 0:64], zr1[:],
                        start=True, stop=True, tile_position=(0, 64)), "t", 1)
                    p.wait("vector", "t", tZB)
                    p.run("vector", lambda E: E.tensor_copy(zb_sb[:], pB[:, 1536:2048]))
                    p.serialize("vector")
                    p.run("vector", lambda E, qb=qbase, h=hp: E.tensor_mul(
                        oT[0:64, h, qb:qb + 512], pB[0:64, 0:512], zb_sb[0:64, :]))
                    vEV_prev[0] = p.inc("vector", lambda E, qb=qbase, h=hp: E.tensor_mul(
                        oT[64:128, h, qb:qb + 512], pB[64:128, 512:1024],
                        zb_sb[64:128, :]), "v", 1)
            if t_oT is not None:
                p.wait(STOREQ, "v")
                p.inc(STOREQ, lambda E, t=t_oT: E.dma_start(
                    out=t[:], in_=oT[:].rearrange("p a b -> p (a b)")), "stD", 16)
            p.barrier()

        p.es_qk.close()
        with ExitStack() as es35:
            sb35 = lambda name, shape, dt=F32: _ts(nc, es35.enter_context(
                nc.sbuf_tensor(name, list(shape), dt)), name)
            ow3 = sb35("ow3", (128, NDC * D), BF16)
            gW = p.inc(CASTQ, lambda E: E.dma_start(
                out=ow3[:].rearrange("p (dc i) -> p dc i", dc=NDC),
                in_=owT_d.rearrange("(dc p) i -> p dc i", p=128)), "gW", 16)
            p.wait("tensor", "gW", gW)
            ow3v = ow3[:].rearrange("p (dc i) -> p dc i", dc=NDC)
            t_attn = tap("attn", (128, NMT * D), BF16)
            for mt in range(NMT):
                ps = pA[:, 0:1024] if mt % 2 == 0 else pA[:, 1024:2048]
                p.wait("tensor", "v")
                for dc in range(NDC):
                    for seg in range(2):
                        p.run("tensor", lambda E, d=dc, s_=seg, m=mt, pp=ps,
                              first=(dc == 0), last=(dc == NDC - 1):
                              E.matmul(pp[:, s_ * 512:(s_ + 1) * 512],
                                       oT[:, d, m * 128:(m + 1) * 128],
                                       ow3v[:, d, s_ * 512:(s_ + 1) * 512],
                                       start=first, stop=last))
                tMM = p.mark("tensor", "t")
                p.wait("vector", "t", tMM)
                p.inc("vector", lambda E, m=mt, pp=ps: E.tensor_add(
                    attn_acc[:, m, :], pp[:], obB[:]), "v", 1)
            if t_attn is not None:
                p.wait(STOREQ, "v")
                p.inc(STOREQ, lambda E, t=t_attn: E.dma_start(
                    out=t[:], in_=attn_acc[:].rearrange("p a b -> p (a b)")), "stD", 16)
            p.barrier()
        es_att.close()
        p.es_vh.close()

    if phases >= 4:
        with ExitStack() as es4a:
            sb4 = lambda name, shape, dt=F32: _ts(nc, es4a.enter_context(
                nc.sbuf_tensor(name, list(shape), dt)), name)
            xqr = [sb4(f"xqr{c}", (128, M), BF16) for c in range(NDC)]
            w1q = sb4("w1q", (128, NDC * DH), BF16)
            w1qv = w1q[:].rearrange("p (dc c) -> p dc c", dc=NDC)
            alphaB = sb4("alphaB", (128, M))
            macc = sb4("macc", (128, M))
            rhi = sb4("rhi", (1, M), BF16)
            rlo = sb4("rlo", (1, M), BF16)
            es4q = ExitStack()
            sb4q = lambda name, shape, dt=F32: _ts(nc, es4q.enter_context(
                nc.sbuf_tensor(name, list(shape), dt)), name)
            wst = [sb4q(f"wst{i}", (128, DH // 2)) for i in range(4)]
            thrP = sb4q("thrP", (128, 1)); thrN = sb4q("thrN", (128, 1))
            g1t = sb4q("g1t", (128, DH // 2))

            for dc in range(NDC):
                p.inc(SYNCQ, lambda E, d=dc: E.dma_start(
                    out=xqr[d][:], in_=xqT_scr[d * 128:(d + 1) * 128, :]), "sXQ", 16)
            sXQ = p.val("sXQ")
            p.run("vector", lambda E: E.tensor_scalar(thrP[:], wts["w1si"][:], 0.5, None, ALU.mult))
            p.serialize("vector")
            p.inc("vector", lambda E: E.tensor_scalar(thrN[:], thrP[:], -1.0, None, ALU.mult), "v", 1)
            p.serialize("vector")
            p.run("vector", lambda E: E.tensor_copy(rhi[:], inv1_row[:]))
            p.serialize("vector")
            vHL = p.inc("vector", lambda E: E.tensor_tensor(
                rlo[:], inv1_row[:], rhi[:], ALU.subtract), "v", 1)
            p.serialize("vector")
            p.wait("tensor", "v", vHL)
            for seg in range(4):
                p.run("tensor", lambda E, s_=seg: E.matmul(
                    pB[:, s_ * 512:(s_ + 1) * 512], ones1rb[:],
                    rhi[0:1, s_ * 512:(s_ + 1) * 512], start=True, stop=False))
                p.run("tensor", lambda E, s_=seg: E.matmul(
                    pB[:, s_ * 512:(s_ + 1) * 512], ones1rb[:],
                    rlo[0:1, s_ * 512:(s_ + 1) * 512], start=False, stop=True))
            tAB = p.mark("tensor", "t")
            p.wait("vector", "t", tAB)
            gAB = p.inc("vector", lambda E: E.tensor_scalar(
                alphaB[:], pB[:, :], wts["w1si"][:], -1.0, ALU.mult, ALU.mult), "v", 1)
            last_w4 = [0, 0, 0, 0]
            for dc in range(NDC):
                for hh in range(2):
                    k4 = (2 * dc + hh) % 4
                    buf = wst[k4]
                    p.wait(SYNCQ, "v", last_w4[k4])
                    sW = p.inc(SYNCQ, lambda E, b=buf, d=dc, h=hh: E.dma_start(
                        out=b[:], in_=w1T_d[d * 128:(d + 1) * 128,
                                           h * (DH // 2):(h + 1) * (DH // 2)]), "sW1", 16)
                    p.wait("vector", "sW1", sW)
                    p.run("vector", lambda E, b=buf: E.tensor_scalar(
                        g1t[:], b[:], thrP[:], None, ALU.is_gt))
                    last_w4[k4] = p.inc(
                        "vector", lambda E, b=buf, d=dc, h=hh: E.scalar_tensor_tensor(
                            w1qv[:, d, h * (DH // 2):(h + 1) * (DH // 2)],
                            b[:], thrN[:], g1t[:], ALU.is_lt, ALU.subtract), "v", 1)
            es4q.close()
            h_sb = [sb4(f"h_sb{i}", (128, M + 2)) for i in range(2)]
            s_sb = [sb4(f"s_sb{i}", (128, M)) for i in range(2)]
            g1t_ = sb4("g1t_", (128, M))
            ab_sb = g1t_
            c_sb = s_sb
            p.run("vector", lambda E: E.memset(macc[:], 0.0))
            for i in range(2):
                p.run("vector", lambda E, b=h_sb[i]: E.memset(b[:, 0:1], 0.0))
                p.run("vector", lambda E, b=h_sb[i]: E.memset(b[:, M + 1:M + 2], 0.0))
            vINIT = p.mark("vector", "v")
            p.wait("tensor", "v", vINIT)
            p.wait("tensor", "sXQ", sXQ)
            p.wait("tensor", "v", gAB)

            t_hsb = tap("hsb", (DH, M))
            t_ssb = tap("ssb", (DH, M))
            last_stH = [0, 0]
            last_vHE = [0, gAB]
            last_sC1 = [0, 0]
            last_sAB2 = [0]
            for cc in range(NCC):
                par = cc % 2
                ps = pA if par == 0 else pB
                p.wait("tensor", "v", last_vHE[par])
                for dc in range(NDC):
                    for seg in range(4):
                        p.run("tensor", lambda E, d=dc, s_=seg, c=cc, pp=ps,
                              first=(dc == 0), last=(dc == NDC - 1):
                              E.matmul(pp[:, s_ * 512:(s_ + 1) * 512],
                                       w1qv[:, d, c * 128:(c + 1) * 128],
                                       xqr[d][:, s_ * 512:(s_ + 1) * 512],
                                       start=first, stop=last))
                tMM = p.mark("tensor", "t")
                p.wait("vector", "t", tMM)
                p.wait("vector", "s", last_sC1[par])
                if t_hsb is not None:
                    p.wait("vector", "stD")
                vHE = p.inc("vector", lambda E, pp=ps, b=h_sb[par]: E.tensor_mul(
                    b[:, 1:M + 1], pp[:], alphaB[:]), "v", 1)
                last_vHE[par] = vHE
                if last_sAB2[0] > 0:
                    p.wait("vector", "s", last_sAB2[0])
                    p.inc("vector", lambda E: E.tensor_tensor(
                        macc[:], macc[:], ab_sb[:], ALU.max), "v", 1)
                    last_sAB2[0] = 0
                p.wait("scalar", "v", vHE)
                p.wait("scalar", "stH", last_stH[par])
                if t_ssb is not None:
                    p.wait("scalar", "stD")
                sC1 = p.inc("scalar", lambda E, b=h_sb[par], o=c_sb[par], c=cc:
                            E.activation(o[:], b[:, 0:M], AF.Identity,
                                         scale=convw[:, c:c + 1],
                                         bias=convb[:, c:c + 1]), "s", 1)
                last_sC1[par] = sC1
                p.wait("vector", "s", sC1)
                p.run("vector", lambda E, b=h_sb[par], o=c_sb[par], c=cc:
                      E.scalar_tensor_tensor(
                          g1t_[:], b[:, 1:M + 1], convw[:, NCC + c:NCC + c + 1],
                          o[:], ALU.mult, ALU.add))
                vC3 = p.inc("vector", lambda E, b=h_sb[par], o=c_sb[par], c=cc:
                            E.scalar_tensor_tensor(
                                o[:], b[:, 2:M + 2],
                                convw[:, 2 * NCC + c:2 * NCC + c + 1],
                                g1t_[:], ALU.mult, ALU.add), "v", 1)
                p.wait("scalar", "v", vC3)
                sSI = p.inc("scalar", lambda E, o=c_sb[par]:
                            E.activation(o[:], o[:], AF.Silu), "s", 1)
                sAB2 = p.inc("scalar", lambda E, s_=s_sb[par]: E.activation(
                    ab_sb[:], s_[:], AF.Abs), "s", 1)
                last_sAB2[0] = sAB2
                p.wait(SYNCQ, "s", sSI)
                last_stH[par] = p.inc(SYNCQ, lambda E, s_=s_sb[par], c=cc: E.dma_start(
                    out=h2s_scr[c * 128:(c + 1) * 128, :], in_=s_[:]), "stH", 16)
                if t_hsb is not None:
                    p.inc(STOREQ, lambda E, b=h_sb[par], c=cc, t=t_hsb: E.dma_start(
                        out=t[c * 128:(c + 1) * 128, :], in_=b[:, 1:M + 1]), "stD", 16)
                if t_ssb is not None:
                    p.inc(STOREQ, lambda E, s_=s_sb[par], c=cc, t=t_ssb: E.dma_start(
                        out=t[c * 128:(c + 1) * 128, :], in_=s_[:]), "stD", 16)
            p.wait("vector", "s", last_sAB2[0])
            p.inc("vector", lambda E: E.tensor_tensor(
                macc[:], macc[:], ab_sb[:], ALU.max), "v", 1)
            amax2c = _ts(nc, es.enter_context(nc.sbuf_tensor(
                "amax2c", [128, NMT], F32, side="right")), "amax2c")
            vMF = p.val("v")
            p.wait("tensor", "v", vMF)
            for mt in range(NMT):
                bk = mt % 4
                p.wait("tensor", "v")
                p.run("tensor", lambda E, m=mt, b=bk: E.transpose(
                    pB[:, b * 512:b * 512 + 128], macc[:, m * 128:(m + 1) * 128],
                    id32[:]))
                tTR = p.mark("tensor", "t")
                p.wait("vector", "t", tTR)
                p.inc("vector", lambda E, m=mt, b=bk: E.reduce_max(
                    amax2c[:, m:m + 1], pB[:, b * 512:b * 512 + 128], AXX), "v", 1)
                p.serialize("vector")
            p.barrier()

        with ExitStack() as es4b:
            sb4b = lambda name, shape, dt=F32: _ts(nc, es4b.enter_context(
                nc.sbuf_tensor(name, list(shape), dt)), name)
            w2q = sb4b("w2q", (128, NCC * D), BF16)
            w2qv = w2q[:].rearrange("p (cc i) -> p cc i", cc=NCC)
            sc1B = sb4b("sc1B", (128, M)); sc2B = sb4b("sc2B", (128, M))
            inv2c = sb4b("inv2c", (128, NMT)); a2cols = sb4b("a2cols", (128, NMT))
            sc2c = sb4b("sc2c", (128, NMT))
            h2l = [sb4b(f"h2l{i}", (128, 512)) for i in range(4)]
            u2 = sb4b("u2", (128, 2, 512))
            ra_ = sb4b("ra_", (128, 2, 512)); rg_ = sb4b("rg_", (128, 2, 512))
            h2q = sb4b("h2q", (128, NCC, 512), BF16)
            t1m = sb4b("t1m", (128, D)); t2m = t1m
            xre = [u2[:].rearrange("p b c -> p (b c)"),
                   ra_[:].rearrange("p b c -> p (b c)")]
            u1v = t1m[:].rearrange("p (b c) -> p b c", b=2)
            ysb = [sb4b(f"ysb{i}", (128, D)) for i in range(2)]
            es4w = ExitStack()
            sb4w = lambda name, shape, dt=F32: _ts(nc, es4w.enter_context(
                nc.sbuf_tensor(name, list(shape), dt)), name)
            w2st = [sb4w(f"w2st{i}", (128, D)) for i in range(2)]
            thr2P = sb4w("thr2P", (128, 1)); thr2N = sb4w("thr2N", (128, 1))
            g2t = sb4w("g2t", (128, D))

            p.run("vector", lambda E: E.tensor_scalar(thr2P[:], wts["w2si"][:], 0.5, None, ALU.mult))
            p.serialize("vector")
            p.inc("vector", lambda E: E.tensor_scalar(thr2N[:], thr2P[:], -1.0, None, ALU.mult), "v", 1)
            p.serialize("vector")
            last_w3 = [0, 0]
            for cc in range(NCC):
                k3 = cc % 2
                buf = w2st[k3]
                p.wait(SYNCQ, "v", last_w3[k3])
                sW = p.inc(SYNCQ, lambda E, b=buf, c=cc: E.dma_start(
                    out=b[:], in_=w2T_d[c * 128:(c + 1) * 128, :]), "sW2", 16)
                p.wait("vector", "sW2", sW)
                p.run("vector", lambda E, b=buf: E.tensor_scalar(
                    g2t[:], b[:], thr2P[:], None, ALU.is_gt))
                last_w3[k3] = p.inc(
                    "vector", lambda E, b=buf, c=cc: E.scalar_tensor_tensor(
                        w2qv[:, c, :], b[:], thr2N[:], g2t[:], ALU.is_lt,
                        ALU.subtract), "v", 1)
            es4w.close()
            es_row = ExitStack()
            sc2row = None
            if "sc2r" in debug_taps:
                sc2row = _ts(nc, es_row.enter_context(nc.sbuf_tensor(
                    "sc2row", [1, M], F32, side="right")), "sc2row")
            p.run("vector", lambda E: E.tensor_scalar(amax2c[:], amax2c[:], 1e-5, None, ALU.max))
            p.serialize("vector")
            p.run("vector", lambda E: E.reciprocal(sc2c[:], amax2c[:]))
            p.serialize("vector")
            p.run("vector", lambda E: E.tensor_scalar(sc2c[:], sc2c[:], 127.0, None, ALU.mult))
            p.serialize("vector")
            p.run("vector", lambda E: E.reciprocal(inv2c[:], sc2c[:]))
            p.serialize("vector")
            vC2 = p.inc("vector", lambda E: E.tensor_scalar(
                a2cols[:], inv2c[:], wts["w2si"][:], -1.0, ALU.mult, ALU.mult), "v", 1)
            p.serialize("vector")
            p.wait("tensor", "v", vC2)
            for mt in range(NMT):
                p.run("tensor", lambda E, m=mt: E.matmul(
                    pB[0:1, m * 128:(m + 1) * 128], sc2c[:, m:m + 1], id32[:],
                    start=(m % 4 == 0), stop=(m % 4 == 3)))
            tSR = p.mark("tensor", "t")
            p.wait("vector", "t", tSR)
            r2hi = _ts(nc, es_row.enter_context(nc.sbuf_tensor(
                "r2hi", [1, M], BF16, side="right")), "r2hi")
            r2lo = _ts(nc, es_row.enter_context(nc.sbuf_tensor(
                "r2lo", [1, M], BF16, side="right")), "r2lo")
            if sc2row is not None:
                p.inc("vector", lambda E: E.tensor_copy(sc2row[:], pB[0:1, :]), "v", 1)
            p.run("vector", lambda E: E.tensor_copy(r2hi[:], pB[0:1, :]))
            p.serialize("vector")
            vHL2 = p.inc("vector", lambda E: E.tensor_tensor(
                r2lo[:], pB[0:1, :], r2hi[:], ALU.subtract), "v", 1)
            p.serialize("vector")
            p.wait("tensor", "v", vHL2)
            for seg in range(4):
                p.run("tensor", lambda E, s_=seg: E.matmul(
                    pA[:, s_ * 512:(s_ + 1) * 512], ones1rb[:],
                    r2hi[0:1, s_ * 512:(s_ + 1) * 512], start=True, stop=False))
                p.run("tensor", lambda E, s_=seg: E.matmul(
                    pA[:, s_ * 512:(s_ + 1) * 512], ones1rb[:],
                    r2lo[0:1, s_ * 512:(s_ + 1) * 512], start=False, stop=True))
            tSB = p.mark("tensor", "t")
            p.wait("vector", "t", tSB)
            p.run("vector", lambda E: E.tensor_scalar(sc1B[:], pA[:, :], w1det[:], None, ALU.mult))
            vSC = p.inc("vector", lambda E: E.tensor_scalar(
                sc2B[:], pA[:, :], w2det[:], None, ALU.mult), "v", 1)
            p.wait("tensor", "v", vSC)
            es_row.close()

            t_h2q = tap("h2q", (DH, M), BF16)
            t_h2all = tap("h2all", (DH, M))
            t_raall = tap("raall", (DH, M))
            t_h2l0 = tap("h2l0", (128, 512))
            t_u1 = tap("u1d", (128, 512))
            t_sc1B = tap("sc1B", (128, M))
            t_sc2c = tap("sc2c", (128, NMT))
            t_sc2r = tap("sc2r", (1, M))
            if t_sc2c is not None:
                p.wait(STOREQ, "v", vSC)
                p.inc(STOREQ, lambda E, t=t_sc2c: E.dma_start(out=t[:], in_=sc2c[:]), "stD", 16)
            if t_sc2r is not None:
                p.wait(STOREQ, "v", vSC)
                p.inc(STOREQ, lambda E, t=t_sc2r: E.dma_start(out=t[:], in_=sc2row[:]), "stD", 16)
            t_am2 = tap("am2", (128, NMT))
            if t_sc1B is not None:
                p.wait(STOREQ, "v", vSC)
                p.inc(STOREQ, lambda E, t=t_sc1B: E.dma_start(out=t[:], in_=sc1B[:]), "stD", 16)
            if t_am2 is not None:
                p.wait(STOREQ, "v", vSC)
                p.inc(STOREQ, lambda E, t=t_am2: E.dma_start(out=t[:], in_=amax2c[:]), "stD", 16)
            last_stY = [0, 0]
            last_q4 = [0, 0, 0, 0]
            last_sME = [0, 0]
            for mb in range(4):
                mlo = mb * 512
                p.wait("vector", "t")
                p.wait("vector", "s", last_sME[0])
                p.wait("vector", "s", last_sME[1])
                if t_h2q is not None:
                    p.wait("vector", "stD")
                pend = None
                vQtr = {}
                for cc in range(NCC):
                    b4 = cc % 4
                    p2 = cc % 2
                    p.wait(SYNCQ, "v", last_q4[b4])
                    if t_h2all is not None:
                        p.wait(SYNCQ, "stD")
                    sH = p.inc(SYNCQ, lambda E, b=h2l[b4], c=cc, m=mlo: E.dma_start(
                        out=b[:], in_=h2s_scr[c * 128:(c + 1) * 128, m:m + 512]), "sH2", 16)
                    if t_h2all is not None:
                        p.wait(STOREQ, "sH2", sH)
                        p.inc(STOREQ, lambda E, t=t_h2all, b=h2l[b4], c=cc, m=mlo:
                              E.dma_start(out=t[c * 128:(c + 1) * 128, m:m + 512],
                                          in_=b[:]), "stD", 16)
                    p.wait("vector", "sH2", sH)
                    if cc == 0:
                        p.wait("vector", "v", vSC)
                    p.run("vector", lambda E, b=h2l[b4], m=mlo, q=p2: E.tensor_mul(
                        u1v[:, q, :], b[:], sc1B[:, m:m + 512]))
                    vU = p.inc("vector", lambda E, b=h2l[b4], m=mlo, q=p2: E.tensor_mul(
                        u2[:, q, :], b[:], sc2B[:, m:m + 512]), "v", 1)
                    last_q4[b4] = vU
                    p.wait("scalar", "v")
                    p.run("scalar", lambda E, q=p2: E.activation(
                        ra_[:, q, :], u1v[:, q, :], AF.Identity, bias=cst["c23c"][:]))
                    p.run("scalar", lambda E, q=p2: E.activation(
                        ra_[:, q, :], ra_[:, q, :], AF.Identity, bias=cst["n23c"][:]))
                    p.run("scalar", lambda E, q=p2: E.activation(
                        rg_[:, q, :], u2[:, q, :], AF.Identity, bias=cst["c52c"][:]))
                    sQ = p.inc("scalar", lambda E, q=p2: E.activation(
                        rg_[:, q, :], rg_[:, q, :], AF.Identity, bias=cst["n52c"][:]),
                        "s", 1)
                    if pend is not None:
                        pc, psQ = pend
                        p.wait("vector", "s", psQ)
                        vAdd = p.inc("vector", lambda E, c=pc, q=pc % 2: E.tensor_add(
                            h2q[:, c, :], ra_[:, q, :], rg_[:, q, :]), "v", 1)
                        if pc % 8 == 7:
                            vQtr[pc // 8] = vAdd
                    pend = (cc, sQ)
                pc, psQ = pend
                p.wait("vector", "s", psQ)
                p.inc("vector", lambda E, c=pc, q=pc % 2: E.tensor_add(
                    h2q[:, c, :], ra_[:, q, :], rg_[:, q, :]), "v", 1)
                vQ = p.val("v")
                if t_h2q is not None:
                    p.wait(STOREQ, "v", vQ)
                    for cc in range(NCC):
                        p.inc(STOREQ, lambda E, c=cc, m=mlo, t=t_h2q: E.dma_start(
                            out=t[c * 128:(c + 1) * 128, m:m + 512], in_=h2q[:, c, :]), "stD", 16)
                for mi in range(4):
                    mt = 4 * mb + mi
                    par = mt % 2
                    ps = pA[:, 0:1024] if par == 0 else pA[:, 1024:2048]
                    p.wait(SYNCQ, "v")
                    sXR = p.inc(SYNCQ, lambda E, b=xre[par], m=mt: E.dma_start(
                        out=b, in_=x_d[m * 128:(m + 1) * 128, :]), "sXR", 16)
                    p.wait("tensor", "s", last_sME[par])
                    for g in range(4):
                        p.wait("tensor", "v", vQtr[g] if g < 3 else vQ)
                        for cc in range(g * 8, (g + 1) * 8):
                            for seg in range(2):
                                p.run("tensor", lambda E, c=cc, s_=seg, mi_=mi,
                                      pp=ps, first=(cc == 0), last=(cc == NCC - 1):
                                      E.matmul(pp[:, s_ * 512:(s_ + 1) * 512],
                                               h2q[:, c, mi_ * 128:(mi_ + 1) * 128],
                                               w2qv[:, c, s_ * 512:(s_ + 1) * 512],
                                               start=first, stop=last))
                    tMM = p.mark("tensor", "t")
                    p.wait("scalar", "t", tMM)
                    p.wait("scalar", "v")
                    sME = p.inc("scalar", lambda E, pp=ps, m=mt: E.activation(
                        t1m[:], pp[:], AF.Identity, scale=a2cols[:, m:m + 1]), "s", 1)
                    last_sME[par] = sME
                    p.wait("vector", "s", sME)
                    p.wait("vector", "sXR", sXR)
                    p.wait("vector", "stY", last_stY[par])
                    p.run("vector", lambda E, b=xre[par]: E.tensor_add(t2m[:], t1m[:], b))
                    vY = p.inc("vector", lambda E, m=mt, o=ysb[par]: E.tensor_add(
                        o[:], t2m[:], attn_acc[:, m, :]), "v", 1)
                    p.wait(STOREQ, "v", vY)
                    last_stY[par] = p.inc(STOREQ, lambda E, o=ysb[par], m=mt:
                                          E.dma_start(out=y_d[m * 128:(m + 1) * 128, :],
                                                      in_=o[:]), "stY", 16)
            t_h2fin = tap("h2fin", (DH, M))
            if t_h2fin is not None:
                p.wait(SYNCQ, "v")
                p.wait(SYNCQ, "stY")
                for c in range(NCC):
                    p.inc(SYNCQ, lambda E, cc_=c, t=t_h2fin: E.dma_start(
                        out=t[cc_ * 128:(cc_ + 1) * 128, :],
                        in_=h2s_scr[cc_ * 128:(cc_ + 1) * 128, :]), "stD", 16)
            p.barrier()
    else:
        yz = sb("yz", (128, D))
        p.run("vector", lambda E: E.memset(yz[:], 0.0))
        vyz = p.mark("vector", "v")
        p.wait(STOREQ, "v", vyz)
        for mt in range(NMT):
            p.inc(STOREQ, lambda E, m=mt: E.dma_start(
                out=y_d[m * 128:(m + 1) * 128, :], in_=yz[:]), "stY", 16)

    p.wait(STOREQ, "stY", p.val("stY"))
    p.wait(STOREQ, "stD", p.val("stD"))

    def _replay(E, stream):
        for item in stream:
            fn, updates = item
            inst = fn(E)
            if updates:
                for u in updates:
                    inst = inst.then_inc(u[0], u[1])

    with nc.Block() as block:
        @block.sync
        def _(E):
            _replay(E, p.streams["sync"])

        @block.tensor
        def _(E):
            _replay(E, p.streams["tensor"])

        @block.vector
        def _(E):
            _replay(E, p.streams["vector"])

        @block.scalar
        def _(E):
            _replay(E, p.streams["scalar"])

        @block.gpsimd
        def _(E):
            _replay(E, p.streams["gpsimd"])

    es.close()
    return nc


_PI = np.concatenate([np.arange(0, D, 2), np.arange(1, D, 2)])


def _prep_inputs(inputs):
    ca = np.ascontiguousarray
    wq, wk, wv = (np.asarray(inputs[k]) for k in
                  ("L0_attn_wq", "L0_attn_wk", "L0_attn_wv"))
    ipw = np.asarray(inputs["L0_attn_in_proj_w"])
    ipb = np.asarray(inputs["L0_attn_in_proj_b"])
    wqi, wki, wvi = ipw[0:D], ipw[D:2 * D], ipw[2 * D:3 * D]
    bqi, bki, bvi = ipb[0:D], ipb[D:2 * D], ipb[2 * D:3 * D]
    cosT, sinT = _host_tables()
    maskT, id32, identbf, ones11, sel2 = _host_consts()
    common = {
        "wqT": ca(wq[_PI, :].T), "wkT": ca(wk[_PI, :].T), "wvT": ca(wv[_PI, :].T),
        "wqiT": ca(wqi[:, _PI].T), "wkiT": ca(wki[:, _PI].T), "wviT": ca(wvi[:, _PI].T),
        "d_bqi": ca(bqi.reshape(NDC, 128).T), "d_bki": ca(bki.reshape(NDC, 128).T),
        "d_bvi": ca(bvi),
        "owT": ca(np.asarray(inputs["L0_attn_out_w"]).T),
        "d_ob": ca(np.asarray(inputs["L0_attn_out_b"])),
        "w1T": ca(np.asarray(inputs["L0_ffn_w1"]).T),
        "w2T": ca(np.asarray(inputs["L0_ffn_w2"]).T),
        "d_convw": ca(np.asarray(inputs["L0_ffn_conv_w"])[:, 0, :]
                      .reshape(NCC, 128, 3).transpose(1, 2, 0).reshape(128, 3 * NCC)),
        "d_convb": ca(np.asarray(inputs["L0_ffn_conv_b"]).reshape(NCC, 128).T),
        "ropec": cosT, "ropes": sinT,
        "d_maskT": maskT, "d_id32": id32, "d_idbf": identbf,
        "d_ones11": ones11, "d_sel2": sel2,
        "d_ones1r": np.ones((1, 128), dtype=np.float32),
        "d_ones1rb": np.ones((1, 128), dtype=ml_dtypes.bfloat16),
        "d_onescol": np.ones((128, 1), dtype=np.float32),
    }
    x = np.asarray(inputs["L0_attn_x"])
    return [dict(common, x=ca(x[b])) for b in range(8)]


def kernel(**inputs):
    in_maps = _prep_inputs(inputs)
    nc = build_nc(DEBUG_TAPS, PHASES)
    res = run_bass_kernel_spmd(nc, in_maps, core_ids=list(range(8)))
    out = np.stack([np.asarray(res.results[b]["y"]) for b in range(8)], axis=0)
    if DEBUG_TAPS:
        kernel.taps = [{k: np.asarray(v) for k, v in r.items() if k.startswith("tap_")}
                       for r in res.results]
    return out
